# version 09
# baseline (speedup 1.0000x reference)
import os
import numpy as np
import concourse.bass as bass
import concourse.mybir as mybir
from concourse.bass_utils import run_bass_kernel_spmd

F32 = mybir.dt.float32
BF16 = mybir.dt.bfloat16
AF = mybir.ActivationFunctionType
ALU = mybir.AluOpType

H = 16
DH = 64
DM = 1024
DF = 4096
NCORES = 8
EPS_LN = 1e-5


class Em:

    ENGINES = ("tensor", "vector", "scalar", "sync", "gpsimd")

    TOTAL_FAMS = ("c_gp", "c_sync")

    def __init__(self):
        self.ops = []
        self.res_w = {}
        self.res_r = {}
        self.last_pe = None
        self.forced = {}

    def op(self, eng, fn, r=(), w=(), dma=False, fam=None, pe_sync=False):
        idx = len(self.ops)
        assert (fam is not None) == dma
        deps = set()
        if pe_sync and self.last_pe is not None:
            deps.add(self.last_pe)
            self.forced[idx] = self.last_pe
        for res in r:
            lw = self.res_w.get(res)
            if lw is not None:
                deps.add(lw)
        for res in w:
            lw = self.res_w.get(res)
            if lw is not None:
                deps.add(lw)
            deps.update(self.res_r.get(res, {}).values())
        rkey = eng if fam is None else ("dma", fam)
        for res in r:
            self.res_r.setdefault(res, {})[rkey] = idx
        for res in w:
            self.res_w[res] = idx
            self.res_r[res] = {}
        if eng == "tensor" and fam is None:
            self.last_pe = idx
        self.ops.append((eng, fn, deps, fam))
        return idx

    def finalize(self, nc, block):
        ops = self.ops
        fam_ord = {}
        fam_cnt = {}
        for i, (eng, fn, deps, fam) in enumerate(ops):
            if fam is not None:
                fam_ord[i] = fam_cnt.get(fam, 0)
                fam_cnt[fam] = fam_ord[i] + 1
        marked = set()
        for i, (eng, fn, deps, fam) in enumerate(ops):
            for d in deps:
                deng, _, _, dfam = ops[d]
                if dfam is None:
                    if not (deng == eng == "tensor") or self.forced.get(i) == d:
                        marked.add(d)
        mark_ord = {}
        cnt = {e: 0 for e in self.ENGINES}
        per_eng = {e: [] for e in self.ENGINES}
        for i, (eng, fn, deps, fam) in enumerate(ops):
            per_eng[eng].append(i)
            if i in marked:
                cnt[eng] += 1
                mark_ord[i] = cnt[eng]
        self._sem_ctx = []
        sems = {}
        fsems = {}
        for e in self.ENGINES:
            cm = nc.semaphore(f"es_{e}")
            self._sem_ctx.append(cm)
            sems[e] = cm.__enter__()
        for fam in fam_cnt:
            cm = nc.semaphore(f"df_{fam}")
            self._sem_ctx.append(cm)
            fsems[fam] = cm.__enter__()

        def body_for(ename):
            def body(engine):
                wm = {}
                for i in per_eng[ename]:
                    eng, fn, deps, fam = ops[i]
                    need = {}
                    for d in deps:
                        deng, _, _, dfam = ops[d]
                        if dfam is not None:
                            key = ("d", dfam)
                            if dfam in self.TOTAL_FAMS:
                                val = 16 * fam_cnt[dfam]
                            else:
                                val = 16 * (fam_ord[d] + 1)
                        else:
                            if deng == ename == "tensor" and self.forced.get(i) != d:
                                continue
                            key = ("e", deng)
                            val = mark_ord[d]
                        if need.get(key, -1) < val:
                            need[key] = val
                    for key in sorted(need, key=str):
                        val = need[key]
                        if wm.get(key, -1) >= val:
                            continue
                        wm[key] = val
                        engine.wait_ge((fsems[key[1]] if key[0] == "d" else sems[key[1]]), val)
                    inst = fn(engine)
                    if fam is not None:
                        inst.then_inc(fsems[fam], 16)
                    elif i in marked:
                        inst.then_inc(sems[ename], 1)
            return body

        for ename in self.ENGINES:
            getattr(block, ename)(body_for(ename))


def host_constants():
    c = {}
    ar = np.arange(64)
    a128 = np.arange(128) % 64
    c["c_id32"] = np.eye(128, dtype=np.float32)
    c["c_idbf"] = np.eye(128, dtype=np.float32)
    c["c_maskun"] = (ar[None, :] < a128[:, None]) * np.float32(1)
    c["c_maskpr"] = (ar[None, :] > a128[:, None]) * np.float32(1)
    c["c_maskpri"] = (ar[None, :] >= a128[:, None]) * np.float32(1)
    ind = np.zeros((16, 16, 64), np.float32)
    for j in range(16):
        ind[j, j, :] = 1.0
    c["c_ind64"] = ind.reshape(16, 1024)
    c["c_nonel"] = np.tile(np.where(np.arange(128) < 64, -1.0, 0.0).astype(np.float32), (16, 1))
    c["c_noneh"] = np.tile(np.where(np.arange(128) >= 64, -1.0, 0.0).astype(np.float32), (16, 1))
    b16 = np.zeros((16, 8, 128), np.float32)
    for blk in range(8):
        b16[2 * blk, blk, 0:64] = 1.0
        b16[2 * blk + 1, blk, 64:128] = 1.0
    c["c_b16"] = b16.reshape(16, 1024)
    sq = np.zeros((128, 8, 16), np.float32)
    for blk in range(8):
        sq[0:64, blk, 2 * blk] = 1.0
        sq[64:128, blk, 2 * blk + 1] = 1.0
    c["c_sq16"] = sq.reshape(128, 128)
    par = np.zeros((16, 128), np.float32)
    for m in range(16):
        par[m, 0:64] = 1.0 if m % 2 == 0 else 0.0
        par[m, 64:128] = 0.0 if m % 2 == 0 else 1.0
    c["c_parity"] = par
    pr = np.zeros((16, 8, 8), np.float32)
    for m in range(16):
        pr[m, :, m // 2] = 1.0
    c["c_pairrep"] = pr.reshape(16, 64)
    c["c_onesrow"] = np.ones((1, 128), np.float32)
    c["c_onescol"] = np.ones((128, 1), np.float32)
    return c

BF_CONSTS = ["c_idbf", "c_maskun", "c_maskpr", "c_maskpri", "c_ind64",
             "c_nonel", "c_noneh", "c_b16", "c_sq16", "c_parity",
             "c_pairrep", "c_onescol"]
F32_CONSTS = ["c_id32", "c_onesrow"]

WEIGHTS = [("wqT", DM, DM), ("wkT", DM, DM), ("wvT", DM, DM), ("wgT", DM, DM),
           ("woT", DM, DM), ("w1T", DM, DF), ("w2T", DM, DF),
           ("h1T", DF, DF), ("h2T", DF, DF), ("w3T", DF, DM)]


SKIP = set(os.environ.get("KSKIP", "").split(",")) - {""}


def build_program(T=8192, CT=512):
    nc = bass.Bass("TRN2", target_bir_lowering=False)
    NTILE = T // CT
    NDBL = CT // 128
    cc = host_constants()

    xt = nc.dram_tensor("xt", [DM, T], F32, kind="ExternalInput").ap()
    yt = nc.dram_tensor("yt", [DM, T], F32, kind="ExternalOutput").ap()
    w_in = {nm: nc.dram_tensor(nm, [a, b], F32, kind="ExternalInput").ap()
            for nm, a, b in WEIGHTS}
    w_in["wab"] = nc.dram_tensor("wab", [DM, 32], F32, kind="ExternalInput").ap()
    sc_in = {nm: nc.dram_tensor(nm, list(shp), F32, kind="ExternalInput").ap()
             for nm, shp in [("s1s", (128, 32)), ("s2s", (128, 32)),
                             ("hs1", (128, 32)), ("hs2", (128, 32)),
                             ("w3s", (128, 8)), ("bacol", (16, CT)), ("bbcol", (16, CT))]}
    cst_in = {nm: nc.dram_tensor(nm, list(cc[nm].shape), F32, kind="ExternalInput").ap()
              for nm in BF_CONSTS + F32_CONSTS}
    scr = {nm.replace("T", "S"): nc.dram_tensor(nm.replace("T", "S"), [a, b], BF16).ap()
           for nm, a, b in WEIGHTS}

    em = Em()
    _ctx = []

    def sb(name, shape, dt):
        t = nc.sbuf_tensor(name, shape, dt)
        _ctx.append(t)
        return t.__enter__()

    def psum(name, shape, dt):
        t = nc.psum_tensor(name, shape, dt)
        _ctx.append(t)
        return t.__enter__()

    K = {}
    for nm in BF_CONSTS:
        K[nm] = sb("sb_" + nm, list(cc[nm].shape), BF16)
    for nm in F32_CONSTS:
        K[nm] = sb("sb_" + nm, list(cc[nm].shape), F32)
    S = {nm: sb("sb_" + nm, list(ap.shape), F32) for nm, ap in sc_in.items()}
    wabsb = sb("wabsb", [128, 256], BF16)
    wslot = [sb(f"wslot{i}", [128, 2048], BF16) for i in range(2)]
    ST = sb("ST", [128, 512], F32)
    STbf = sb("STbf", [128, 512], BF16)
    stsc = sb("stsc", [128, 512], F32)
    outb = [sb(f"outb{i}", [128, CT], F32) for i in range(2)]

    xbf = sb("xbf", [128, 8 * CT], BF16)
    xn = sb("xn", [128, 8 * CT], BF16)
    qn = sb("qn", [128, 8 * CT], BF16)
    qg = sb("qg", [128, 8 * CT], BF16)
    kn = sb("kn", [128, 8 * CT], BF16)
    kb = sb("kb", [128, 8 * CT], BF16)
    kgb = sb("kgb", [128, 8 * CT], BF16)
    vb = sb("vb", [128, 8 * CT], BF16)
    gate = sb("gate", [128, 8 * CT], BF16)

    sqscr = sb("sqscr", [128, CT], BF16)
    sqscr2 = sb("sqscr2", [128, CT], BF16)
    silscr = sb("silscr", [128, CT], BF16)
    mbs = sb("mbs", [128, CT], BF16)
    rbs = sb("rbs", [128, CT], BF16)
    rowsA = sb("rowsA", [1, 2 * CT], F32)
    rowsB = sb("rowsB", [1, 2 * CT], F32)
    rowf1 = sb("rowf1", [16, 6 * CT], F32)
    rowf2 = sb("rowf2", [16, 5 * CT], F32)
    rG = (rowf1[:, 0:CT], ("rf1", 0))
    rgv = (rowf1[:, CT:2 * CT], ("rf1", 1))
    rgr = (rowf1[:, 2 * CT:3 * CT], ("rf1", 2))
    rbr = (rowf1[:, 3 * CT:4 * CT], ("rf1", 3))
    rla = (rowf1[:, 4 * CT:5 * CT], ("rf1", 4))
    rgp = (rowf1[:, 5 * CT:6 * CT], ("rf1", 5))
    r2 = [(rowf2[:, i * CT:(i + 1) * CT], ("rf2", i)) for i in range(5)]
    rowbA = sb("rowbA", [16, 4 * CT], BF16)
    rowbB = sb("rowbB", [16, 2 * CT], BF16)
    rowg = sb("rowg", [16, 2 * CT], BF16)
    mov2 = [sb(f"mov2_{i}", [16, 1024], BF16) for i in range(2)]
    gC = sb("gC", [128, 64], F32)
    SMTb = sb("SMTb", [128, 16], BF16)
    epsb = sb("epsb", [1, 1], F32)

    escr = sb("escr", [128, 1024], F32)
    Eun = sb("Eun", [128, 1024], BF16)
    Eprs = sb("Eprs", [128, 1024], BF16)
    Epri = sb("Epri", [128, 1024], BF16)
    Aun_s = sb("Aun_s", [128, 1024], BF16)
    Aprs = sb("Aprs", [128, 1024], BF16)
    Akqs = sb("Akqs", [128, 1024], BF16)
    P0s = sb("P0s", [128, 1024], BF16)
    Xp = {m: sb(f"Xp{m}", [128, 1024], BF16) for m in (1, 2, 4, 8, 16, 32)}
    Xua = sb("Xua", [128, 1024], BF16)
    Xub = sb("Xub", [128, 1024], BF16)
    Zs = [sb(f"Zs{i}", [128, 1024], BF16) for i in range(2)]
    Wgt = sb("Wgt", [128, 1024], BF16)
    VbTs = sb("VbTs", [128, 1024], BF16)
    Ktoks = sb("Ktoks", [128, 1024], BF16)

    pA = psum("pA", [128, 1024], F32)
    pB = psum("pB", [128, 1024], F32)
    pC = psum("pC", [128, 1024], F32)
    pS = psum("pS", [128, 512], F32)
    ptr = psum("ptr", [128, 1024], BF16)
    RA = ("pA0", "pA1")
    RB = ("pB0", "pB1")
    RC = ("pC0", "pC1")

    O = em.op

    def blk(t8, b):
        return t8[:, b * CT:(b + 1) * CT]

    def mk_map(pairs):
        out = []
        for t8, rname in pairs:
            nb = t8.shape[1] // CT
            for b in range(nb):
                res = (rname, b) if rname in ("kn", "kb", "kgb", "vb", "gate", "xbf") else rname
                out.append((t8[:, b * CT:(b + 1) * CT], res))
        return out

    u1map = mk_map([(kn, "kn"), (kb, "kb"), (kgb, "kgb"), (vb, "vb")])
    h2map = mk_map([(gate, "gate"), (xbf, "xbf"), (Xp[1], "Xp1"), (Xp[2], "Xp2"),
                    (Xp[4], "Xp4"), (Xp[8], "Xp8"), (Xp[16], "Xp16"), (Xp[32], "Xp32"),
                    (Zs[0], "Zs0"), (Zs[1], "Zs1")])[:32]

    for nm in BF_CONSTS:
        O("gpsimd", (lambda e, nm=nm: e.dma_start(K[nm][:, :], cst_in[nm][:, :])),
          w=(nm,), dma=True, fam="c_gp")
    for nm in F32_CONSTS:
        O("sync", (lambda e, nm=nm: e.dma_start(K[nm][:, :], cst_in[nm][:, :])),
          w=(nm,), dma=True, fam="c_sync")
    for nm in S:
        O("sync", (lambda e, nm=nm: e.dma_start(S[nm][:, :], sc_in[nm][:, :])),
          w=(nm,), dma=True, fam="c_sync")
    O("gpsimd", lambda e: e.dma_start(
        wabsb[:, :].rearrange("p (j c) -> p j c", j=8),
        w_in["wab"][:, :].rearrange("(j p) c -> p j c", p=128)),
      w=("wabsb",), dma=True, fam="c_gp")
    O("vector", lambda e: e.memset(ST[:, :], 0.0), w=("ST",))
    O("vector", lambda e: e.memset(STbf[:, :], 0.0), w=("STbf",))
    O("vector", lambda e: e.memset(epsb[:, :], EPS_LN), w=("epsb",))

    SLOTW = 8 * CT
    ppbufs = [(gate, tuple(("gate", b) for b in range(8))),
              (xbf, tuple(("xbf", b) for b in range(8)))]
    ppi = 0
    for nm, din, dout in WEIGHTS:
        dst = nm.replace("T", "S")
        for rr in range(din // 128):
            for c0 in range(0, dout, SLOTW):
                cw = min(SLOTW, dout - c0)
                buf, bres = ppbufs[ppi % 2]
                ppi += 1
                pfam = "g" if ppi % 2 == 1 else "x"
                O("gpsimd", (lambda e, nm=nm, rr=rr, c0=c0, cw=cw, buf=buf:
                             e.dma_start(buf[:, 0:cw],
                                         w_in[nm][128 * rr:128 * rr + 128, c0:c0 + cw])),
                  w=bres, dma=True, fam="ppg" + pfam)
                O("sync", (lambda e, dst=dst, rr=rr, c0=c0, cw=cw, buf=buf:
                           e.dma_start(scr[dst][128 * rr:128 * rr + 128, c0:c0 + cw],
                                       buf[:, 0:cw])),
                  r=bres, w=((dst, rr),), dma=True, fam="ppo" + pfam)

    def ln_stats(src_t, src_res):
        for b in range(8):
            O("vector", (lambda e, b=b: e.tensor_mul(sqscr[:, :], blk(src_t, b), blk(src_t, b))),
              r=((src_res, b),), w=("sqscr",))
            O("tensor", (lambda e, b=b: e.matmul(pS[0:1, 0:CT], K["c_onescol"][:, :],
                                                 blk(src_t, b), start=(b == 0), stop=(b == 7))),
              r=((src_res, b), "c_onescol"), w=("pS",))
            O("tensor", (lambda e, b=b: e.matmul(pC[0:1, 0:CT], K["c_onescol"][:, :],
                                                 sqscr[:, :], start=(b == 0), stop=(b == 7))),
              r=("sqscr", "c_onescol"), w=RC)
        O("vector", lambda e: e.tensor_single_scalar(rowsA[:, 0:CT], pS[0:1, 0:CT],
                                                     1.0 / DM, ALU.mult),
          r=("pS",), w=("rmean",))
        O("vector", lambda e: e.tensor_mul(rowsB[:, 0:CT], rowsA[:, 0:CT], rowsA[:, 0:CT]),
          r=("rmean",), w=("rmsq",))
        O("vector", lambda e: e.scalar_tensor_tensor(rowsB[:, CT:2 * CT], pC[0:1, 0:CT],
                                                     1.0 / DM, rowsB[:, 0:CT],
                                                     ALU.mult, ALU.subtract),
          r=RC + ("rmsq",), w=("rvar",))
        O("scalar", lambda e: e.activation(rowsA[:, CT:2 * CT], rowsB[:, CT:2 * CT],
                                           AF.Ln, bias=epsb[:, :]),
          r=("rvar", "epsb"), w=("rln",))
        O("scalar", lambda e: e.activation(rowsB[:, 0:CT], rowsA[:, CT:2 * CT],
                                           AF.Exp, scale=-0.5),
          r=("rln",), w=("rrstd",))
        O("tensor", lambda e: e.matmul(pS[0:128, 0:CT], K["c_onesrow"][:, :],
                                       rowsA[:, 0:CT], start=True, stop=True),
          r=("rmean", "c_onesrow"), w=("pS",))
        O("tensor", lambda e: e.matmul(pC[0:128, 0:CT], K["c_onesrow"][:, :],
                                       rowsB[:, 0:CT], start=True, stop=True),
          r=("rrstd", "c_onesrow"), w=RC)
        O("vector", lambda e: e.tensor_copy(mbs[:, :], pS[0:128, 0:CT]), r=("pS",), w=("mbs",))
        O("vector", lambda e: e.tensor_copy(rbs[:, :], pC[0:128, 0:CT]), r=RC, w=("rbs",))

    def ln_apply(src_t, src_res, dst_t, dst_res, resid_t=None, resid_res=None):
        for b in range(8):
            scr2 = sqscr2
            O("vector", (lambda e, b=b: e.tensor_sub(scr2[:, :], blk(src_t, b), mbs[:, :])),
              r=((src_res, b), "mbs"), w=("sqscr2",))
            if resid_t is None:
                O("vector", (lambda e, b=b: e.tensor_mul(blk(dst_t, b), scr2[:, :], rbs[:, :])),
                  r=("sqscr2", "rbs"), w=((dst_res, b),))
            else:
                O("vector", (lambda e, b=b: e.tensor_mul(scr2[:, :], scr2[:, :], rbs[:, :])),
                  r=("sqscr2", "rbs"), w=("sqscr2",))
                O("vector", (lambda e, b=b: e.tensor_add(blk(dst_t, b), scr2[:, :],
                                                         blk(resid_t, b))),
                  r=("sqscr2", (resid_res, b)), w=((dst_res, b),))

    wsl = [0]

    def stream_mm(wS, din, dout, pmaps, mov_ap, mov_res, epi):
        nob = dout // 128
        nsb = din // 512
        G = len(pmaps)
        for og0 in range(0, nob, G):
            ogn = min(G, nob - og0)
            width = ogn * 128
            for sbk in range(nsb):
                slot = wslot[wsl[0] % 2]
                sres = f"wslot{wsl[0] % 2}"
                wsl[0] += 1
                O("sync", (lambda e, sbk=sbk, og0=og0, slot=slot, width=width, wS=wS:
                           e.dma_start(slot[:, 0:4 * width].rearrange("p (jj o) -> p jj o", jj=4),
                                       scr[wS][512 * sbk:512 * sbk + 512,
                                               128 * og0:128 * og0 + width]
                                       .rearrange("(jj p) o -> p jj o", p=128))),
                  r=tuple((wS, 4 * sbk + j) for j in range(4)), w=(sres,), dma=True, fam=sres)
                for jj in range(4):
                    j = 4 * sbk + jj
                    for oi in range(ogn):
                        pap, bres = pmaps[oi]
                        stat = slot[:, jj * width + oi * 128: jj * width + (oi + 1) * 128]
                        O("tensor", (lambda e, pap=pap, stat=stat, mv=mov_ap(j),
                                     first=(sbk == 0 and jj == 0),
                                     last=(sbk == nsb - 1 and jj == 3):
                                     e.matmul(pap, stat, mv, start=first, stop=last)),
                          r=(sres, mov_res(j)), w=bres)
            for oi in range(ogn):
                epi(og0 + oi, *pmaps[oi])

    def pmap4():
        return [(pA[:, 0:CT], ("pA0",)), (pA[:, 512:512 + CT], ("pA1",)),
                (pB[:, 0:CT], ("pB0",)), (pB[:, 512:512 + CT], ("pB1",))]

    def logsig(psrc, psrc_res, bias, dst, dst_res):
        (zap, zr), (sap, sr), (map_, mr) = r2[0], r2[1], r2[2]
        O("vector", (lambda e: e.tensor_copy(zap, psrc)),
          r=(psrc_res,), w=(zr,))
        O("vector", (lambda e: e.tensor_add(zap, zap, S[bias][:, :])),
          r=(zr, bias), w=(zr,))
        O("scalar", lambda e: e.activation(sap, zap, AF.Abs),
          r=(zr,), w=(sr,))
        O("scalar", lambda e: e.activation(sap, sap, AF.Exp, scale=-1.0),
          r=(sr,), w=(sr,))
        O("scalar", lambda e: e.activation(sap, sap, AF.Ln, bias=1.0),
          r=(sr,), w=(sr,))
        O("vector", lambda e: e.tensor_single_scalar(map_, zap, 0.0, ALU.min),
          r=(zr,), w=(mr,))
        O("vector", (lambda e: e.tensor_sub(dst, map_, sap)),
          r=(mr, sr), w=(dst_res,))

    seg = CT // 64

    for ti in range(NTILE):
        ts = ti * CT
        O("gpsimd", (lambda e, ts=ts: e.dma_start(
            xbf[:, :].rearrange("p (b t) -> p b t", b=8),
            xt[:, ts:ts + CT].rearrange("(b p) t -> p b t", p=128))),
          w=tuple(("xbf", b) for b in range(8)), dma=True, fam="xin")

        ln_stats(xbf, "xbf")
        ln_apply(xbf, "xbf", xn, "xn")

        def epi_copy(dst_t, dres):
            def f(o, pap, bres):
                O("scalar", (lambda e, o=o, pap=pap: e.copy(blk(dst_t, o), pap)),
                  r=bres, w=((dres, o),))
            return f

        def epi_silu(dst_t, dres):
            def f(o, pap, bres):
                O("scalar", (lambda e, o=o, pap=pap: e.activation(blk(dst_t, o), pap, AF.Silu)),
                  r=bres, w=((dres, o),))
            return f

        mov_xn = lambda j: blk(xn, j)
        res_xn = lambda j: ("xn", j)
        stream_mm("wqS", DM, DM, pmap4(), mov_xn, res_xn, epi_copy(qg, "qg"))
        stream_mm("wkS", DM, DM, pmap4(), mov_xn, res_xn, epi_copy(kn, "kn"))
        stream_mm("wvS", DM, DM, pmap4(), mov_xn, res_xn, epi_copy(vb, "vb"))
        stream_mm("wgS", DM, DM, pmap4(), mov_xn, res_xn, epi_silu(gate, "gate"))

        for j in range(8):
            O("tensor", (lambda e, j=j: e.matmul(pA[0:16, 0:CT], wabsb[:, j * 32:j * 32 + 16],
                                                 blk(xn, j), start=(j == 0), stop=(j == 7))),
              r=(("xn", j), "wabsb"), w=("pA0",))
            O("tensor", (lambda e, j=j: e.matmul(pB[0:16, 0:CT],
                                                 wabsb[:, j * 32 + 16:j * 32 + 32],
                                                 blk(xn, j), start=(j == 0), stop=(j == 7))),
              r=(("xn", j), "wabsb"), w=("pB0",))

        logsig(pA[0:16, 0:CT], "pA0", "bacol", rla[0], rla[1])
        O("vector", lambda e: e.tensor_single_scalar(rla[0], rla[0], -60.0, ALU.max),
          r=(rla[1],), w=(rla[1],))
        logsig(pB[0:16, 0:CT], "pB0", "bbcol", r2[3][0], r2[3][1])
        O("scalar", lambda e: e.activation(rbr[0], r2[3][0], AF.Exp),
          r=(r2[3][1],), w=(rbr[1],))
        O("vector", lambda e: e.tensor_copy(rG[0], rla[0]),
          r=(rla[1],), w=(rG[1],))
        for sh in (1, 2, 4, 8, 16, 32):
            O("vector", (lambda e, sh=sh: e.tensor_add(
                rgp[0].rearrange("p (s c) -> p s c", c=64)[:, :, sh:64],
                rG[0].rearrange("p (s c) -> p s c", c=64)[:, :, sh:64],
                rG[0].rearrange("p (s c) -> p s c", c=64)[:, :, 0:64 - sh])),
              r=(rG[1],), w=(rgp[1],))
            O("vector", (lambda e, sh=sh: e.tensor_copy(
                rgp[0].rearrange("p (s c) -> p s c", c=64)[:, :, 0:sh],
                rG[0].rearrange("p (s c) -> p s c", c=64)[:, :, 0:sh])),
              r=(rG[1],), w=(rgp[1],))
            O("vector", lambda e: e.tensor_copy(rG[0], rgp[0]),
              r=(rgp[1],), w=(rG[1],))
        O("scalar", lambda e: e.activation(rgv[0], rG[0], AF.Exp),
          r=(rG[1],), w=(rgv[1],))
        gend = rG[0].rearrange("p (s c) -> p s c", c=64)[:, :, 63:64] \
            .broadcast_to([16, seg, 64])
        O("vector", lambda e: e.tensor_sub(
            rgr[0].rearrange("p (s c) -> p s c", c=64),
            rG[0].rearrange("p (s c) -> p s c", c=64), gend),
          r=(rG[1],), w=(rgr[1],))
        O("scalar", lambda e: e.activation(rgr[0], rgr[0], AF.Exp, scale=-1.0),
          r=(rgr[1],), w=(rgr[1],))
        O("vector", lambda e: e.tensor_copy(rowg[:, 0:CT], rG[0]),
          r=(rG[1],), w=("Ghib",))
        O("vector", lambda e: e.tensor_sub(rowg[:, CT:2 * CT], rG[0],
                                           rowg[:, 0:CT]),
          r=(rG[1], "Ghib"), w=("Glob",))
        gvend = rgv[0].rearrange("p (s c) -> p s c", c=64)[:, :, 63:64] \
            .broadcast_to([16, seg, 8])
        O("vector", (lambda e: e.tensor_mul(
            mov2[0][:, 0:8 * seg].rearrange("p (s c) -> p s c", c=8), gvend,
            K["c_pairrep"][:, 0:8 * seg].rearrange("p (s c) -> p s c", c=8))),
          r=(rgv[1], "c_pairrep"), w=(("m2", 0),))
        O("tensor", lambda e: e.matmul(pS[0:128, 0:8 * seg], K["c_parity"][:, :],
                                       mov2[0][:, 0:8 * seg], start=True, stop=True),
          r=(("m2", 0), "c_parity"), w=("pS",))
        O("vector", lambda e: e.tensor_copy(gC[:, 0:8 * seg], pS[0:128, 0:8 * seg]),
          r=("pS",), w=("gC",))

        for raw_t, raw_res, pap, bres, dst, dres in (
                (qg, "qg", pS[0:16, 0:CT], ("pS",), r2[0][0], r2[0][1]),
                (kn, "kn", pC[0:16, 0:CT], RC, r2[1][0], r2[1][1])):
            for b in range(8):
                O("vector", (lambda e, b=b, raw_t=raw_t: e.tensor_mul(
                    sqscr[:, :], blk(raw_t, b), blk(raw_t, b))),
                  r=((raw_res, b),), w=("sqscr",))
                O("tensor", (lambda e, b=b, pap=pap: e.matmul(
                    pap, K["c_sq16"][:, b * 16:b * 16 + 16], sqscr[:, :],
                    start=(b == 0), stop=(b == 7))),
                  r=("sqscr", "c_sq16"), w=bres)
            O("scalar", (lambda e, pap=pap, dst=dst: e.activation(dst, pap, AF.Ln)),
              r=bres, w=(dres,))
            O("scalar", (lambda e, dst=dst: e.activation(dst, dst, AF.Exp, scale=-0.5)),
              r=(dres,), w=(dres,))
            O("vector", (lambda e, dst=dst: e.tensor_single_scalar(dst, dst, 1e6, ALU.min)),
              r=(dres,), w=(dres,))
        O("vector", lambda e: e.tensor_mul(r2[2][0], r2[0][0], rgv[0]),
          r=(r2[0][1], rgv[1]), w=(r2[2][1],))
        O("vector", lambda e: e.scalar_tensor_tensor(r2[3][0], r2[1][0],
                                                     -1.0, rbr[0],
                                                     ALU.mult, ALU.mult),
          r=(r2[1][1], rbr[1]), w=(r2[3][1],))
        O("vector", lambda e: e.tensor_mul(r2[4][0], r2[3][0], rgv[0]),
          r=(r2[3][1], rgv[1]), w=(r2[4][1],))
        rowsrc = [(r2[0][1], r2[0][0], rowbA[:, 0:CT]),
                  (r2[2][1], r2[2][0], rowbA[:, CT:2 * CT]),
                  (r2[1][1], r2[1][0], rowbA[:, 2 * CT:3 * CT]),
                  (r2[3][1], r2[3][0], rowbA[:, 3 * CT:4 * CT]),
                  (r2[4][1], r2[4][0], rowbB[:, 0:CT]),
                  (rbr[1], rbr[0], rowbB[:, CT:2 * CT])]
        for res, src, dst in rowsrc:
            O("vector", (lambda e, src=src, dst=dst: e.tensor_copy(dst, src)),
              r=(res,), w=((res, "bf"),))
        rowap = {}
        rowres = {}
        for nm, (res, _, dst) in zip(("rnq", "rg", "rnk", "rkb", "rkg", "brow"), rowsrc):
            rowap[nm] = dst
            rowres[nm] = (res, "bf")

        for b in range(8):
            def bc(res, pap, bres):
                O("tensor", (lambda e, res=res, pap=pap, b=b: e.matmul(
                    pap, K["c_b16"][:, b * 128:(b + 1) * 128], rowap[res],
                    start=True, stop=True)),
                  r=(rowres[res], "c_b16"), w=bres)
            bc("rnq", pS[0:128, 0:CT], ("pS",))
            O("vector", (lambda e, b=b: e.scalar_tensor_tensor(
                blk(qn, b), pS[0:128, 0:CT], 1.0, blk(qg, b), ALU.mult, ALU.mult)),
              r=("pS", ("qg", b)), w=(("qn", b),))
            bc("rg", pC[0:128, 0:CT], RC)
            O("vector", (lambda e, b=b: e.scalar_tensor_tensor(
                blk(qg, b), pC[0:128, 0:CT], 1.0, blk(qg, b), ALU.mult, ALU.mult)),
              r=RC + (("qg", b),), w=(("qg", b),))
            bc("rkb", pS[0:128, 0:CT], ("pS",))
            O("vector", (lambda e, b=b: e.scalar_tensor_tensor(
                blk(kb, b), pS[0:128, 0:CT], 1.0, blk(kn, b), ALU.mult, ALU.mult)),
              r=("pS", ("kn", b)), w=(("kb", b),))
            bc("rkg", pC[0:128, 0:CT], RC)
            O("vector", (lambda e, b=b: e.scalar_tensor_tensor(
                blk(kgb, b), pC[0:128, 0:CT], 1.0, blk(kn, b), ALU.mult, ALU.mult)),
              r=RC + (("kn", b),), w=(("kgb", b),))
            bc("rnk", pS[0:128, 0:CT], ("pS",))
            O("vector", (lambda e, b=b: e.scalar_tensor_tensor(
                blk(kn, b), pS[0:128, 0:CT], 1.0, blk(kn, b), ALU.mult, ALU.mult)),
              r=("pS", ("kn", b)), w=(("kn", b),))
            bc("brow", pC[0:128, 0:CT], RC)
            O("vector", (lambda e, b=b: e.scalar_tensor_tensor(
                blk(vb, b), pC[0:128, 0:CT], 1.0, blk(vb, b), ALU.mult, ALU.mult)),
              r=RC + (("vb", b),), w=(("vb", b),))

        for d in range(NDBL if "delta" not in SKIP else 0):
            t0 = d * 128

            def hsl(t8, h, c):
                return blk(t8, h // 2)[64 * (h % 2):64 * (h % 2) + 64,
                                       t0 + 64 * c: t0 + 64 * c + 64]

            O("tensor", (lambda e, t0=t0: e.transpose(
                pS[0:128, 0:16], rowf1[:, 2 * CT + t0:2 * CT + t0 + 128],
                K["c_id32"][0:16, 0:16])),
              r=(rgr[1], "c_id32"), w=("pS",))
            O("vector", lambda e: e.tensor_copy(SMTb[:, 0:16], pS[0:128, 0:16]),
              r=("pS",), w=("SMTb",))

            for g2 in range(2 if "dpm" not in SKIP and "dem" not in SKIP else 0):
                for rres, rap in (("Ghib", rowg[:, t0:t0 + 128]),
                                  ("Glob", rowg[:, CT + t0:CT + t0 + 128])):
                    O("tensor", (lambda e, g2=g2, rap=rap, first=(rres == "Ghib"):
                                 e.matmul(pB[:, 512 * g2:512 * g2 + 512], rap,
                                          K["c_ind64"][:, 512 * g2:512 * g2 + 512],
                                          start=first, stop=False)),
                      r=(rres, "c_ind64"), w=(RB[g2],))
            for half, nmsk in ((0, "c_nonel"), (1, "c_noneh")):
                for hl, rres in ((0, "Ghib"), (1, "Glob")):
                    c0r = (0, CT)[hl]
                    src = rowg[:, c0r + t0 + 64 * half: c0r + t0 + 64 * half + 64] \
                        .unsqueeze(1).broadcast_to([16, 16, 64])
                    O("vector", (lambda e, hl=hl, src=src: e.tensor_mul(
                        mov2[hl][:, :].rearrange("p (h c) -> p h c", c=64),
                        K["c_ind64"][:, :].rearrange("p (h c) -> p h c", c=64), src)),
                      r=(rres, "c_ind64"), w=(("m2", hl),))
                for g2 in range(2):
                    for hl in (0, 1):
                        last = (half == 1 and hl == 1)
                        O("tensor", (lambda e, g2=g2, hl=hl, nmsk=nmsk, last=last:
                                     e.matmul(pB[:, 512 * g2:512 * g2 + 512], K[nmsk][:, :],
                                              mov2[hl][:, 512 * g2:512 * g2 + 512],
                                              start=False, stop=last)),
                          r=(("m2", hl), nmsk), w=(RB[g2],))
            O("vector", lambda e: e.tensor_single_scalar(escr[:, :], pB[:, :], 0.0, ALU.min),
              r=RB, w=("escr",))
            O("scalar", lambda e: e.activation(Eun[:, :], escr[:, :], AF.Exp),
              r=("escr",), w=("EunR",))
            O("vector", lambda e: e.tensor_scalar(escr[:, :], pB[:, :], -1.0, 0.0,
                                                  ALU.mult, ALU.min),
              r=RB, w=("escr",))
            O("scalar", lambda e: e.activation(Eprs[:, :], escr[:, :], AF.Exp),
              r=("escr",), w=("EprR",))
            O("vector", lambda e: e.tensor_mul(
                Epri[:, :].rearrange("p (h c) -> p h c", c=64),
                Eprs[:, :].rearrange("p (h c) -> p h c", c=64),
                K["c_maskpri"][:, :].unsqueeze(1).broadcast_to([128, 16, 64])),
              r=("EprR", "c_maskpri"), w=("Epri",))
            O("vector", lambda e: e.tensor_mul(
                Eprs[:, :].rearrange("p (h c) -> p h c", c=64),
                Eprs[:, :].rearrange("p (h c) -> p h c", c=64),
                K["c_maskpr"][:, :].unsqueeze(1).broadcast_to([128, 16, 64])),
              r=("EprR", "c_maskpr"), w=("Eprs",))
            O("vector", lambda e: e.tensor_mul(
                Eun[:, :].rearrange("p (h c) -> p h c", c=64),
                Eun[:, :].rearrange("p (h c) -> p h c", c=64),
                K["c_maskun"][:, :].unsqueeze(1).broadcast_to([128, 16, 64])),
              r=("EunR", "c_maskun"), w=("Eun",))

            ablocks = [(0, 0), (0, 64), (64, 64), (64, 0)]
            if "dem" in SKIP:
                ablocks = []
            for bi, (hb, cb) in enumerate(ablocks):
                c = cb // 64
                first = True
                for h in range(hb // 64, H, 2):
                    tp = (hb, cb) if (hb or cb) else None
                    bres_h = lambda pair: (pair[0],) if h < 8 else (pair[1],)
                    ps = (bi == 2) and first
                    first = False
                    O("tensor", (lambda e, s=hsl(kb, h, c), m=hsl(kn, h, c), h=h, cb=cb, tp=tp:
                                 e.matmul(pA[cb:cb + 64, 64 * h:64 * h + 64], s, m,
                                          start=True, stop=True, tile_position=tp)),
                      r=(("kb", h // 2), ("kn", h // 2)), w=bres_h(RA), pe_sync=ps)
                    O("tensor", (lambda e, s=hsl(kn, h, c), m=hsl(kb, h, c), h=h, cb=cb, tp=tp:
                                 e.matmul(pC[cb:cb + 64, 64 * h:64 * h + 64], s, m,
                                          start=True, stop=True, tile_position=tp)),
                      r=(("kb", h // 2), ("kn", h // 2)), w=bres_h(RC))
                    O("tensor", (lambda e, s=hsl(kn, h, c), m=hsl(qn, h, c), h=h, cb=cb, tp=tp:
                                 e.matmul(pB[cb:cb + 64, 64 * h:64 * h + 64], s, m,
                                          start=True, stop=True, tile_position=tp)),
                      r=(("kn", h // 2), ("qn", h // 2)), w=bres_h(RB))
            O("vector", lambda e: e.tensor_copy(Aun_s[:, :], pA[:, :]), r=RA, w=("Aun_s",))
            O("scalar", lambda e: e.copy(Aprs[:, :], pC[:, :]), r=RC, w=("Aprs",))
            O("vector", lambda e: e.tensor_copy(Akqs[:, :], pB[:, :]), r=RB, w=("Akqs",))
            O("vector", lambda e: e.tensor_mul(Xp[1][:, :], Aprs[:, :], Eprs[:, :]),
              r=("Aprs", "Eprs"), w=("Xp1",))
            O("vector", lambda e: e.tensor_mul(Xua[:, :], Aun_s[:, :], Eun[:, :]),
              r=("Aun_s", "Eun"), w=("Xua",))
            O("vector", lambda e: e.tensor_mul(P0s[:, :], Akqs[:, :], Epri[:, :]),
              r=("Akqs", "Epri"), w=("P0s",))

            for src_t, sres, dst, dres in (((vb, "vb", VbTs, "VbTs"),
                                            (kn, "kn", Ktoks, "Ktoks"))
                                           if "dtp" not in SKIP else ()):
                for pr in range(8):
                    for c in range(2):
                        cb = 64 * c
                        src = blk(src_t, pr)[:, t0 + 64 * c: t0 + 64 * c + 64]
                        O("tensor", (lambda e, src=src, pr=pr, cb=cb: e.transpose(
                            ptr[cb:cb + 64, 128 * pr:128 * pr + 128], src,
                            K["c_idbf"][:, :],
                            tile_position=(0, cb) if cb else None)),
                          r=((sres, pr), "c_idbf"), w=("ptr",))
                O("vector", (lambda e, dst=dst: e.tensor_copy(dst[:, :], ptr[:, :])),
                  r=("ptr",), w=(dres,))

            prevp, prevu = "Xp1", "Xua"
            prevpt, prevut = Xp[1], Xua
            for m in ((2, 4, 8, 16, 32) if "dsq" not in SKIP else ()):
                for h in range(H):
                    for c in range(2):
                        cb = 64 * c
                        tp = (cb, cb) if cb else None
                        O("tensor", (lambda e, h=h, cb=cb, tp=tp, su=prevut, sp=prevpt:
                                     e.matmul(pA[cb:cb + 64, 64 * h:64 * h + 64],
                                              su[cb:cb + 64, 64 * h:64 * h + 64],
                                              sp[cb:cb + 64, 64 * h:64 * h + 64],
                                              start=True, stop=True, tile_position=tp)),
                          r=(prevu, prevp), w=(RA[0],) if h < 8 else (RA[1],))
                        if m != 32:
                            O("tensor", (lambda e, h=h, cb=cb, tp=tp, su=prevut, sp=prevpt:
                                         e.matmul(pC[cb:cb + 64, 64 * h:64 * h + 64],
                                                  sp[cb:cb + 64, 64 * h:64 * h + 64],
                                                  su[cb:cb + 64, 64 * h:64 * h + 64],
                                                  start=True, stop=True, tile_position=tp)),
                              r=(prevu, prevp), w=(RC[0],) if h < 8 else (RC[1],))
                O("vector", (lambda e, m=m: e.tensor_copy(Xp[m][:, :], pA[:, :])),
                  r=RA, w=(f"Xp{m}",))
                prevp, prevpt = f"Xp{m}", Xp[m]
                if m != 32:
                    nxt = Xub if prevut is Xua else Xua
                    nxtr = "Xub" if prevut is Xua else "Xua"
                    O("scalar", (lambda e, nxt=nxt: e.copy(nxt[:, :], pC[:, :])),
                      r=RC, w=(nxtr,))
                    prevu, prevut = nxtr, nxt

            for c in range(2 if "dzos" not in SKIP else 0):
                cb = 64 * c
                cg = 2 * d + c
                for g2 in range(2):
                    O("tensor", (lambda e, cb=cb, g2=g2: e.matmul(
                        pB[cb:cb + 64, 512 * g2:512 * g2 + 512],
                        K["c_idbf"][cb:cb + 64, cb:cb + 64],
                        VbTs[cb:cb + 64, 512 * g2:512 * g2 + 512],
                        start=True, stop=False, skip_group_check=True,
                        tile_position=(cb, cb) if cb else None)),
                      r=("VbTs", "c_idbf"), w=(RB[g2],))
                for par in (0, 1):
                    first = True
                    for h in range(par, H, 2):
                        hb = 64 * par
                        pr = h // 2
                        ps = first
                        first = False
                        O("tensor", (lambda e, s=hsl(kgb, h, c), hb=hb, pr=pr, cb=cb, h=h:
                                     e.matmul(pB[cb:cb + 64, 64 * h:64 * h + 64], s,
                                              STbf[hb:hb + 64, 64 * pr:64 * pr + 64],
                                              start=False, stop=False, skip_group_check=True,
                                              tile_position=(hb, cb) if (hb or cb) else None)),
                          r=(("kgb", h // 2), "STbf"), w=(RB[0],) if h < 8 else (RB[1],),
                          pe_sync=ps)
                O("vector", (lambda e, cb=cb: e.tensor_copy(Zs[0][cb:cb + 64, :],
                                                            pB[cb:cb + 64, :])),
                  r=RB, w=("Zs0",))
                zi = 0
                for m in (1, 2, 4, 8, 16, 32):
                    mt = Xp[m]
                    for h in range(H):
                        O("tensor", (lambda e, mt=mt, h=h, cb=cb, zi=zi, m=m:
                                     e.matmul(pB[cb:cb + 64, 64 * h:64 * h + 64],
                                              mt[cb:cb + 64, 64 * h:64 * h + 64],
                                              Zs[zi][cb:cb + 64, 64 * h:64 * h + 64],
                                              start=False, stop=False, skip_group_check=True,
                                              tile_position=(cb, cb) if cb else None)),
                          r=(f"Xp{m}", f"Zs{zi}"), w=(RB[0],) if h < 8 else (RB[1],))
                    zi = 1 - zi
                    O("vector", (lambda e, cb=cb, zi=zi: e.tensor_copy(
                        Zs[zi][cb:cb + 64, :], pB[cb:cb + 64, :])),
                      r=RB, w=(f"Zs{zi}",))
                O("vector", (lambda e, cb=cb, zi=zi: e.tensor_mul(
                    Wgt[cb:cb + 64, :].rearrange("p (h i) -> p h i", i=64),
                    Zs[zi][cb:cb + 64, :].rearrange("p (h i) -> p h i", i=64),
                    SMTb[cb:cb + 64, 0:16].unsqueeze(2).broadcast_to([64, 16, 64]))),
                  r=(f"Zs{zi}", "SMTb"), w=("Wgt",))
                for h in range(H):
                    hb = 64 * (h % 2)
                    pr = h // 2
                    col = 512 * c + 64 * pr
                    O("tensor", (lambda e, hb=hb, pr=pr, col=col, m=hsl(qg, h, c), h=h:
                                 e.matmul(pC[hb:hb + 64, col:col + 64],
                                          STbf[hb:hb + 64, 64 * pr:64 * pr + 64], m,
                                          start=(h <= 1), stop=False, skip_group_check=True,
                                          tile_position=(hb, hb) if hb else None)),
                      r=("STbf", ("qg", h // 2)), w=(RC[c],))
                for h in range(H):
                    hb = 64 * (h % 2)
                    pr = h // 2
                    col = 512 * c + 64 * pr
                    O("tensor", (lambda e, hb=hb, cb=cb, col=col, zi=zi, h=h:
                                 e.matmul(pC[hb:hb + 64, col:col + 64],
                                          Zs[zi][cb:cb + 64, 64 * h:64 * h + 64],
                                          P0s[cb:cb + 64, 64 * h:64 * h + 64],
                                          start=False, stop=(h >= H - 2), skip_group_check=True,
                                          tile_position=(cb, hb) if (cb or hb) else None)),
                      r=(f"Zs{zi}", "P0s"), w=(RC[c],))
                for h in range(H):
                    hb = 64 * (h % 2)
                    pr = h // 2
                    O("tensor", (lambda e, hb=hb, pr=pr, cb=cb, h=h: e.matmul(
                        pS[hb:hb + 64, 64 * pr:64 * pr + 64],
                        Ktoks[cb:cb + 64, 128 * pr + hb:128 * pr + hb + 64],
                        Wgt[cb:cb + 64, 64 * h:64 * h + 64],
                        start=(h <= 1), stop=(h >= H - 2), skip_group_check=True,
                        tile_position=(cb, hb) if (cb or hb) else None)),
                      r=("Ktoks", "Wgt"), w=("pS",))
                O("vector", (lambda e, cg=cg: e.tensor_mul(
                    stsc[:, :].rearrange("p (r i) -> p r i", i=64),
                    ST[:, :].rearrange("p (r i) -> p r i", i=64),
                    gC[:, cg * 8:cg * 8 + 8].unsqueeze(2).broadcast_to([128, 8, 64]))),
                  r=("ST", "gC"), w=("stsc",))
                O("vector", lambda e: e.tensor_add(ST[:, :], stsc[:, :], pS[0:128, 0:512]),
                  r=("stsc", "pS"), w=("ST",))
                O("vector", lambda e: e.tensor_copy(STbf[:, :], ST[:, :]),
                  r=("ST",), w=("STbf",))
            for c in range(2):
                ogap = gate[:, :].rearrange("p (pr tt) -> p pr tt", pr=8)[
                    :, :, t0 + 64 * c: t0 + 64 * c + 64]
                pcv = pC[:, 512 * c:512 * c + 512].rearrange("p (pr t) -> p pr t", pr=8)
                O("vector", (lambda e, ogap=ogap, pcv=pcv: e.scalar_tensor_tensor(
                    ogap, pcv, 1.0, ogap, ALU.mult, ALU.mult)),
                  r=(RC[c],) + tuple(("gate", b) for b in range(8)),
                  w=tuple(("gate", b) for b in range(8)))

        stream_mm("woS", DM, DM, pmap4(), lambda j: blk(gate, j), lambda j: ("gate", j),
                  epi_copy(vb, "vb"))
        ln_stats(vb, "vb")
        ln_apply(vb, "vb", xn, "xn", resid_t=xbf, resid_res="xbf")

        ln_stats(xn, "xn")
        ln_apply(xn, "xn", qn, "qn")

        mov_yn = lambda j: blk(qn, j)
        res_yn = lambda j: ("qn", j)
        nob = DF // 128
        for og0 in range(0, nob, 2):
            for wS, pmaps in (("w1S", [(pA[:, 0:CT], ("pA0",)), (pA[:, 512:512 + CT], ("pA1",))]),
                              ("w2S", [(pC[:, 0:CT], ("pC0",)), (pC[:, 512:512 + CT], ("pC1",))])):
                for sbk in range(2):
                    slot = wslot[wsl[0] % 2]
                    sres = f"wslot{wsl[0] % 2}"
                    wsl[0] += 1
                    O("sync", (lambda e, sbk=sbk, og0=og0, slot=slot, wS=wS:
                               e.dma_start(slot[:, 0:1024].rearrange("p (jj o) -> p jj o", jj=4),
                                           scr[wS][512 * sbk:512 * sbk + 512,
                                                   128 * og0:128 * og0 + 256]
                                           .rearrange("(jj p) o -> p jj o", p=128))),
                      r=tuple((wS, 4 * sbk + j) for j in range(4)), w=(sres,), dma=True, fam=sres)
                    for jj in range(4):
                        j = 4 * sbk + jj
                        for oi in range(2):
                            pap, bres = pmaps[oi]
                            stat = slot[:, jj * 256 + oi * 128: jj * 256 + (oi + 1) * 128]
                            O("tensor", (lambda e, pap=pap, stat=stat, mv=mov_yn(j),
                                         first=(sbk == 0 and jj == 0),
                                         last=(sbk == 1 and jj == 3):
                                         e.matmul(pap, stat, mv, start=first, stop=last)),
                              r=(sres, res_yn(j)), w=bres)
            for oi in range(2):
                o = og0 + oi
                papA = (pA[:, 0:CT], pA[:, 512:512 + CT])[oi]
                papC = (pC[:, 0:CT], pC[:, 512:512 + CT])[oi]
                brA = (("pA0",), ("pA1",))[oi]
                brC = (("pC0",), ("pC1",))[oi]
                uap, ures = u1map[o]
                O("scalar", (lambda e, o=o, papA=papA: e.activation(
                    silscr[:, :], papA, AF.Silu, scale=S["s1s"][:, o:o + 1])),
                  r=brA + ("s1s",), w=("silscr",))
                O("vector", (lambda e, o=o, papC=papC, uap=uap: e.scalar_tensor_tensor(
                    uap, papC, S["s2s"][:, o:o + 1], silscr[:, :], ALU.mult, ALU.mult)),
                  r=brC + ("s2s", "silscr"), w=(ures,))

        def epi_gelu(dmap, scal):
            def f(o, pap, bres):
                dap, dres = dmap[o]
                O("scalar", (lambda e, o=o, pap=pap, dap=dap: e.activation(
                    dap, pap, AF.Gelu, scale=S[scal][:, o:o + 1])),
                  r=bres + (scal,), w=(dres,))
            return f

        stream_mm("h1S", DF, DF, pmap4(), lambda j: u1map[j][0], lambda j: u1map[j][1],
                  epi_gelu(h2map, "hs1"))
        stream_mm("h2S", DF, DF, pmap4(), lambda j: h2map[j][0], lambda j: h2map[j][1],
                  epi_gelu(u1map, "hs2"))

        def epi_w3(o, pap, bres):
            O("scalar", (lambda e, o=o, pap=pap: e.mul(blk(qg, o), pap, S["w3s"][:, o:o + 1])),
              r=bres + ("w3s",), w=(("qg", o),))

        stream_mm("w3S", DF, DM, pmap4(), lambda j: u1map[j][0], lambda j: u1map[j][1],
                  epi_w3)

        ln_stats(qg, "qg")
        for b in range(8):
            ob = outb[b % 2]
            obres = f"outb{b % 2}"
            O("vector", (lambda e, b=b: e.tensor_sub(sqscr2[:, :], blk(qg, b), mbs[:, :])),
              r=(("qg", b), "mbs"), w=("sqscr2",))
            O("vector", (lambda e, b=b: e.tensor_mul(sqscr2[:, :], sqscr2[:, :], rbs[:, :])),
              r=("sqscr2", "rbs"), w=("sqscr2",))
            O("vector", (lambda e, b=b, ob=ob: e.tensor_add(ob[:, :], sqscr2[:, :],
                                                            blk(xn, b))),
              r=("sqscr2", ("xn", b)), w=(obres,))
            O("sync", (lambda e, b=b, ob=ob, ts=ts: e.dma_start(
                yt[128 * b:128 * b + 128, ts:ts + CT], ob[:, :])),
              r=(obres,), w=(("ydone", b % 2),), dma=True, fam=f"yo{b % 2}")

    O("sync", lambda e: e.nop(), r=(("ydone", 0), ("ydone", 1)))

    with nc.Block() as block:
        em.finalize(nc, block)
    return nc


_cache = {}


def prep_inputs(inputs, CT=512):
    cc = host_constants()
    base = {}
    for nm, src in [("wqT", "L0_attn_Wq"), ("wkT", "L0_attn_Wk"), ("wvT", "L0_attn_Wv"),
                    ("wgT", "L0_attn_Wg"), ("woT", "L0_attn_Wo"),
                    ("w1T", "L0_ffn_w1_w"), ("w2T", "L0_ffn_w2_w"),
                    ("h1T", "L0_ffn_h1_w"), ("h2T", "L0_ffn_h2_w"),
                    ("w3T", "L0_ffn_w3_w")]:
        base[nm] = np.ascontiguousarray(np.asarray(inputs[src], np.float32).T)
    base["wab"] = np.ascontiguousarray(np.concatenate(
        [np.asarray(inputs["L0_attn_Wa"], np.float32).T,
         np.asarray(inputs["L0_attn_Wb"], np.float32).T], axis=1))
    for nm, src in [("s1s", "L0_ffn_w1_s"), ("s2s", "L0_ffn_w2_s"),
                    ("hs1", "L0_ffn_h1_s"), ("hs2", "L0_ffn_h2_s")]:
        base[nm] = np.ascontiguousarray(np.asarray(inputs[src], np.float32).reshape(32, 128).T)
    base["w3s"] = np.ascontiguousarray(np.asarray(inputs["L0_ffn_w3_s"], np.float32).reshape(8, 128).T)
    base["bacol"] = np.ascontiguousarray(np.broadcast_to(
        np.asarray(inputs["L0_attn_ba"], np.float32).reshape(16, 1), (16, CT)))
    base["bbcol"] = np.ascontiguousarray(np.broadcast_to(
        np.asarray(inputs["L0_attn_bb"], np.float32).reshape(16, 1), (16, CT)))
    for nm in BF_CONSTS + F32_CONSTS:
        base[nm] = cc[nm]
    return base


def kernel(**inputs):
    T, CT = 8192, 512
    if "prog" not in _cache:
        _cache["prog"] = build_program(T, CT)
    nc = _cache["prog"]
    base = prep_inputs(inputs)
    x = np.asarray(inputs["L0_attn_x"], np.float32)
    in_maps = []
    for c in range(NCORES):
        m = dict(base)
        m["xt"] = np.ascontiguousarray(x[c].T)
        in_maps.append(m)
    res = run_bass_kernel_spmd(nc, in_maps, core_ids=list(range(NCORES)))
    out = np.stack([np.asarray(r["yt"]).T for r in res.results], axis=0)
    return out.astype(np.float32)


# version 10
# speedup vs baseline: 1.0725x; 1.0725x over previous
import os
import numpy as np
import concourse.bass as bass
import concourse.mybir as mybir
from concourse.bass_utils import run_bass_kernel_spmd

F32 = mybir.dt.float32
BF16 = mybir.dt.bfloat16
AF = mybir.ActivationFunctionType
ALU = mybir.AluOpType

H = 16
DH = 64
DM = 1024
DF = 4096
NCORES = 8
EPS_LN = 1e-5


class Em:

    ENGINES = ("tensor", "vector", "scalar", "sync", "gpsimd")

    TOTAL_FAMS = ("c_gp", "c_sync")

    def __init__(self):
        self.ops = []
        self.res_w = {}
        self.res_r = {}
        self.last_pe = None
        self.forced = {}

    def op(self, eng, fn, r=(), w=(), dma=False, fam=None, pe_sync=False):
        idx = len(self.ops)
        assert (fam is not None) == dma
        deps = set()
        if pe_sync and self.last_pe is not None:
            deps.add(self.last_pe)
            self.forced[idx] = self.last_pe
        for res in r:
            lw = self.res_w.get(res)
            if lw is not None:
                deps.add(lw)
        for res in w:
            lw = self.res_w.get(res)
            if lw is not None:
                deps.add(lw)
            deps.update(self.res_r.get(res, {}).values())
        rkey = eng if fam is None else ("dma", fam)
        for res in r:
            self.res_r.setdefault(res, {})[rkey] = idx
        for res in w:
            self.res_w[res] = idx
            self.res_r[res] = {}
        if eng == "tensor" and fam is None:
            self.last_pe = idx
        self.ops.append((eng, fn, deps, fam))
        return idx

    def finalize(self, nc, block):
        ops = self.ops
        fam_ord = {}
        fam_cnt = {}
        for i, (eng, fn, deps, fam) in enumerate(ops):
            if fam is not None:
                fam_ord[i] = fam_cnt.get(fam, 0)
                fam_cnt[fam] = fam_ord[i] + 1
        marked = set()
        for i, (eng, fn, deps, fam) in enumerate(ops):
            for d in deps:
                deng, _, _, dfam = ops[d]
                if dfam is None:
                    if not (deng == eng == "tensor") or self.forced.get(i) == d:
                        marked.add(d)
        mark_ord = {}
        cnt = {e: 0 for e in self.ENGINES}
        per_eng = {e: [] for e in self.ENGINES}
        for i, (eng, fn, deps, fam) in enumerate(ops):
            per_eng[eng].append(i)
            if i in marked:
                cnt[eng] += 1
                mark_ord[i] = cnt[eng]
        self._sem_ctx = []
        sems = {}
        fsems = {}
        for e in self.ENGINES:
            cm = nc.semaphore(f"es_{e}")
            self._sem_ctx.append(cm)
            sems[e] = cm.__enter__()
        for fam in fam_cnt:
            cm = nc.semaphore(f"df_{fam}")
            self._sem_ctx.append(cm)
            fsems[fam] = cm.__enter__()

        def body_for(ename):
            def body(engine):
                wm = {}
                for i in per_eng[ename]:
                    eng, fn, deps, fam = ops[i]
                    need = {}
                    for d in deps:
                        deng, _, _, dfam = ops[d]
                        if dfam is not None:
                            key = ("d", dfam)
                            if dfam in self.TOTAL_FAMS:
                                val = 16 * fam_cnt[dfam]
                            else:
                                val = 16 * (fam_ord[d] + 1)
                        else:
                            if deng == ename == "tensor" and self.forced.get(i) != d:
                                continue
                            key = ("e", deng)
                            val = mark_ord[d]
                        if need.get(key, -1) < val:
                            need[key] = val
                    for key in sorted(need, key=str):
                        val = need[key]
                        if wm.get(key, -1) >= val:
                            continue
                        wm[key] = val
                        engine.wait_ge((fsems[key[1]] if key[0] == "d" else sems[key[1]]), val)
                    inst = fn(engine)
                    if fam is not None:
                        inst.then_inc(fsems[fam], 16)
                    elif i in marked:
                        inst.then_inc(sems[ename], 1)
            return body

        for ename in self.ENGINES:
            getattr(block, ename)(body_for(ename))


def host_constants():
    c = {}
    ar = np.arange(64)
    a128 = np.arange(128) % 64
    c["c_id32"] = np.eye(128, dtype=np.float32)
    c["c_idbf"] = np.eye(128, dtype=np.float32)
    c["c_maskun"] = (ar[None, :] < a128[:, None]) * np.float32(1)
    c["c_maskpr"] = (ar[None, :] > a128[:, None]) * np.float32(1)
    c["c_maskpri"] = (ar[None, :] >= a128[:, None]) * np.float32(1)
    ind = np.zeros((16, 16, 64), np.float32)
    for j in range(16):
        ind[j, j, :] = 1.0
    c["c_ind64"] = ind.reshape(16, 1024)
    c["c_nonel"] = np.tile(np.where(np.arange(128) < 64, -1.0, 0.0).astype(np.float32), (16, 1))
    c["c_noneh"] = np.tile(np.where(np.arange(128) >= 64, -1.0, 0.0).astype(np.float32), (16, 1))
    b16 = np.zeros((16, 8, 128), np.float32)
    for blk in range(8):
        b16[2 * blk, blk, 0:64] = 1.0
        b16[2 * blk + 1, blk, 64:128] = 1.0
    c["c_b16"] = b16.reshape(16, 1024)
    sq = np.zeros((128, 8, 16), np.float32)
    for blk in range(8):
        sq[0:64, blk, 2 * blk] = 1.0
        sq[64:128, blk, 2 * blk + 1] = 1.0
    c["c_sq16"] = sq.reshape(128, 128)
    par = np.zeros((16, 128), np.float32)
    for m in range(16):
        par[m, 0:64] = 1.0 if m % 2 == 0 else 0.0
        par[m, 64:128] = 0.0 if m % 2 == 0 else 1.0
    c["c_parity"] = par
    pr = np.zeros((16, 8, 8), np.float32)
    for m in range(16):
        pr[m, :, m // 2] = 1.0
    c["c_pairrep"] = pr.reshape(16, 64)
    c["c_onesrow"] = np.ones((1, 128), np.float32)
    c["c_onescol"] = np.ones((128, 1), np.float32)
    return c

BF_CONSTS = ["c_idbf", "c_maskun", "c_maskpr", "c_maskpri", "c_ind64",
             "c_nonel", "c_noneh", "c_b16", "c_sq16", "c_parity",
             "c_pairrep", "c_onescol"]
F32_CONSTS = ["c_id32", "c_onesrow"]

WEIGHTS = [("wqT", DM, DM), ("wkT", DM, DM), ("wvT", DM, DM), ("wgT", DM, DM),
           ("woT", DM, DM), ("w1T", DM, DF), ("w2T", DM, DF),
           ("h1T", DF, DF), ("h2T", DF, DF), ("w3T", DF, DM)]


SKIP = set(os.environ.get("KSKIP", "").split(",")) - {""}


def build_program(T=8192, CT=512):
    nc = bass.Bass("TRN2", target_bir_lowering=False)
    NTILE = T // CT
    NDBL = CT // 128
    cc = host_constants()

    xt = nc.dram_tensor("xt", [DM, T], F32, kind="ExternalInput").ap()
    yt = nc.dram_tensor("yt", [DM, T], F32, kind="ExternalOutput").ap()
    w_in = {nm: nc.dram_tensor(nm, [a, b], F32, kind="ExternalInput").ap()
            for nm, a, b in WEIGHTS}
    w_in["wab"] = nc.dram_tensor("wab", [DM, 32], F32, kind="ExternalInput").ap()
    sc_in = {nm: nc.dram_tensor(nm, list(shp), F32, kind="ExternalInput").ap()
             for nm, shp in [("s1s", (128, 32)), ("s2s", (128, 32)),
                             ("hs1", (128, 32)), ("hs2", (128, 32)),
                             ("w3s", (128, 8)), ("bacol", (16, CT)), ("bbcol", (16, CT))]}
    cst_in = {nm: nc.dram_tensor(nm, list(cc[nm].shape), F32, kind="ExternalInput").ap()
              for nm in BF_CONSTS + F32_CONSTS}
    scr = {nm.replace("T", "S"): nc.dram_tensor(nm.replace("T", "S"), [a, b], BF16).ap()
           for nm, a, b in WEIGHTS}

    em = Em()
    _ctx = []

    def sb(name, shape, dt):
        t = nc.sbuf_tensor(name, shape, dt)
        _ctx.append(t)
        return t.__enter__()

    def psum(name, shape, dt):
        t = nc.psum_tensor(name, shape, dt)
        _ctx.append(t)
        return t.__enter__()

    K = {}
    for nm in BF_CONSTS:
        K[nm] = sb("sb_" + nm, list(cc[nm].shape), BF16)
    for nm in F32_CONSTS:
        K[nm] = sb("sb_" + nm, list(cc[nm].shape), F32)
    S = {nm: sb("sb_" + nm, list(ap.shape), F32) for nm, ap in sc_in.items()}
    wabsb = sb("wabsb", [128, 256], BF16)
    wslot = [sb(f"wslot{i}", [128, 2048], BF16) for i in range(4)]
    ST = sb("ST", [128, 512], F32)
    STbf = sb("STbf", [128, 512], BF16)
    stsc = sb("stsc", [128, 512], F32)
    outb = [sb(f"outb{i}", [128, CT], F32) for i in range(2)]

    xbf = sb("xbf", [128, 8 * CT], BF16)
    xn = sb("xn", [128, 8 * CT], BF16)
    qn = sb("qn", [128, 8 * CT], BF16)
    qg = sb("qg", [128, 8 * CT], BF16)
    kn = sb("kn", [128, 8 * CT], BF16)
    kb = sb("kb", [128, 8 * CT], BF16)
    kgb = sb("kgb", [128, 8 * CT], BF16)
    vb = sb("vb", [128, 8 * CT], BF16)
    gate = sb("gate", [128, 8 * CT], BF16)

    sqscr = sb("sqscr", [128, CT], BF16)
    sqscr2 = sb("sqscr2", [128, CT], BF16)
    silscr = sb("silscr", [128, CT], BF16)
    mbs = sb("mbs", [128, CT], BF16)
    rbs = sb("rbs", [128, CT], BF16)
    rowsA = sb("rowsA", [1, 2 * CT], F32)
    rowsB = sb("rowsB", [1, 2 * CT], F32)
    rowf1 = sb("rowf1", [16, 6 * CT], F32)
    rowf2 = sb("rowf2", [16, 5 * CT], F32)
    rG = (rowf1[:, 0:CT], ("rf1", 0))
    rgv = (rowf1[:, CT:2 * CT], ("rf1", 1))
    rgr = (rowf1[:, 2 * CT:3 * CT], ("rf1", 2))
    rbr = (rowf1[:, 3 * CT:4 * CT], ("rf1", 3))
    rla = (rowf1[:, 4 * CT:5 * CT], ("rf1", 4))
    rgp = (rowf1[:, 5 * CT:6 * CT], ("rf1", 5))
    r2 = [(rowf2[:, i * CT:(i + 1) * CT], ("rf2", i)) for i in range(5)]
    rowbA = sb("rowbA", [16, 4 * CT], BF16)
    rowbB = sb("rowbB", [16, 2 * CT], BF16)
    rowg = sb("rowg", [16, 2 * CT], BF16)
    mov2 = [sb(f"mov2_{i}", [16, 1024], BF16) for i in range(2)]
    gC = sb("gC", [128, 64], F32)
    SMTb = sb("SMTb", [128, 16], BF16)
    epsb = sb("epsb", [1, 1], F32)

    escr = sb("escr", [128, 1024], F32)
    Eun = sb("Eun", [128, 1024], BF16)
    Eprs = sb("Eprs", [128, 1024], BF16)
    Epri = sb("Epri", [128, 1024], BF16)
    Aun_s = sb("Aun_s", [128, 1024], BF16)
    Aprs = sb("Aprs", [128, 1024], BF16)
    Akqs = sb("Akqs", [128, 1024], BF16)
    P0s = sb("P0s", [128, 1024], BF16)
    Xp = {m: sb(f"Xp{m}", [128, 1024], BF16) for m in (1, 2, 4, 8, 16, 32)}
    Xua = sb("Xua", [128, 1024], BF16)
    Xub = sb("Xub", [128, 1024], BF16)
    Zs = [sb(f"Zs{i}", [128, 1024], BF16) for i in range(2)]
    Wgt = sb("Wgt", [128, 1024], BF16)
    VbTs = sb("VbTs", [128, 1024], BF16)
    Ktoks = sb("Ktoks", [128, 1024], BF16)

    pA = psum("pA", [128, 1024], F32)
    pB = psum("pB", [128, 1024], F32)
    pC = psum("pC", [128, 1024], F32)
    pS = psum("pS", [128, 512], F32)
    ptr = psum("ptr", [128, 1024], BF16)
    RA = ("pA0", "pA1")
    RB = ("pB0", "pB1")
    RC = ("pC0", "pC1")

    O = em.op

    def blk(t8, b):
        return t8[:, b * CT:(b + 1) * CT]

    def mk_map(pairs):
        out = []
        for t8, rname in pairs:
            nb = t8.shape[1] // CT
            for b in range(nb):
                res = (rname, b) if rname in ("kn", "kb", "kgb", "vb", "gate", "xbf") else rname
                out.append((t8[:, b * CT:(b + 1) * CT], res))
        return out

    u1map = mk_map([(kn, "kn"), (kb, "kb"), (kgb, "kgb"), (vb, "vb")])
    h2map = mk_map([(gate, "gate"), (xbf, "xbf"), (Xp[1], "Xp1"), (Xp[2], "Xp2"),
                    (Xp[4], "Xp4"), (Xp[8], "Xp8"), (Xp[16], "Xp16"), (Xp[32], "Xp32"),
                    (Zs[0], "Zs0"), (Zs[1], "Zs1")])[:32]

    for nm in BF_CONSTS:
        O("gpsimd", (lambda e, nm=nm: e.dma_start(K[nm][:, :], cst_in[nm][:, :])),
          w=(nm,), dma=True, fam="c_gp")
    for nm in F32_CONSTS:
        O("sync", (lambda e, nm=nm: e.dma_start(K[nm][:, :], cst_in[nm][:, :])),
          w=(nm,), dma=True, fam="c_sync")
    for nm in S:
        O("sync", (lambda e, nm=nm: e.dma_start(S[nm][:, :], sc_in[nm][:, :])),
          w=(nm,), dma=True, fam="c_sync")
    O("gpsimd", lambda e: e.dma_start(
        wabsb[:, :].rearrange("p (j c) -> p j c", j=8),
        w_in["wab"][:, :].rearrange("(j p) c -> p j c", p=128)),
      w=("wabsb",), dma=True, fam="c_gp")
    O("vector", lambda e: e.memset(ST[:, :], 0.0), w=("ST",))
    O("vector", lambda e: e.memset(STbf[:, :], 0.0), w=("STbf",))
    O("vector", lambda e: e.memset(epsb[:, :], EPS_LN), w=("epsb",))

    SLOTW = 8 * CT
    ppbufs = [(gate, tuple(("gate", b) for b in range(8))),
              (xbf, tuple(("xbf", b) for b in range(8)))]
    ppi = 0
    for nm, din, dout in WEIGHTS:
        dst = nm.replace("T", "S")
        for rr in range(din // 128):
            for c0 in range(0, dout, SLOTW):
                cw = min(SLOTW, dout - c0)
                buf, bres = ppbufs[ppi % 2]
                ppi += 1
                pfam = "g" if ppi % 2 == 1 else "x"
                O("gpsimd", (lambda e, nm=nm, rr=rr, c0=c0, cw=cw, buf=buf:
                             e.dma_start(buf[:, 0:cw],
                                         w_in[nm][128 * rr:128 * rr + 128, c0:c0 + cw])),
                  w=bres, dma=True, fam="ppg" + pfam)
                O("sync", (lambda e, dst=dst, rr=rr, c0=c0, cw=cw, buf=buf:
                           e.dma_start(scr[dst][128 * rr:128 * rr + 128, c0:c0 + cw],
                                       buf[:, 0:cw])),
                  r=bres, w=((dst, rr),), dma=True, fam="ppo" + pfam)

    def ln_stats(src_t, src_res):
        for b in range(8):
            sq = (sqscr, sqscr2)[b % 2]
            sqr = ("sqscr", "sqscr2")[b % 2]
            O("vector", (lambda e, b=b, sq=sq: e.tensor_mul(sq[:, :], blk(src_t, b), blk(src_t, b))),
              r=((src_res, b),), w=(sqr,))
            O("tensor", (lambda e, b=b: e.matmul(pS[0:1, 0:CT], K["c_onescol"][:, :],
                                                 blk(src_t, b), start=(b == 0), stop=(b == 7))),
              r=((src_res, b), "c_onescol"), w=("pS",))
            O("tensor", (lambda e, b=b, sq=sq: e.matmul(pC[0:1, 0:CT], K["c_onescol"][:, :],
                                                 sq[:, :], start=(b == 0), stop=(b == 7))),
              r=(sqr, "c_onescol"), w=RC)
        O("vector", lambda e: e.tensor_single_scalar(rowsA[:, 0:CT], pS[0:1, 0:CT],
                                                     1.0 / DM, ALU.mult),
          r=("pS",), w=("rmean",))
        O("vector", lambda e: e.tensor_mul(rowsB[:, 0:CT], rowsA[:, 0:CT], rowsA[:, 0:CT]),
          r=("rmean",), w=("rmsq",))
        O("vector", lambda e: e.scalar_tensor_tensor(rowsB[:, CT:2 * CT], pC[0:1, 0:CT],
                                                     1.0 / DM, rowsB[:, 0:CT],
                                                     ALU.mult, ALU.subtract),
          r=RC + ("rmsq",), w=("rvar",))
        O("scalar", lambda e: e.activation(rowsA[:, CT:2 * CT], rowsB[:, CT:2 * CT],
                                           AF.Ln, bias=epsb[:, :]),
          r=("rvar", "epsb"), w=("rln",))
        O("scalar", lambda e: e.activation(rowsB[:, 0:CT], rowsA[:, CT:2 * CT],
                                           AF.Exp, scale=-0.5),
          r=("rln",), w=("rrstd",))
        O("tensor", lambda e: e.matmul(pS[0:128, 0:CT], K["c_onesrow"][:, :],
                                       rowsA[:, 0:CT], start=True, stop=True),
          r=("rmean", "c_onesrow"), w=("pS",))
        O("tensor", lambda e: e.matmul(pC[0:128, 0:CT], K["c_onesrow"][:, :],
                                       rowsB[:, 0:CT], start=True, stop=True),
          r=("rrstd", "c_onesrow"), w=RC)
        O("vector", lambda e: e.tensor_copy(mbs[:, :], pS[0:128, 0:CT]), r=("pS",), w=("mbs",))
        O("vector", lambda e: e.tensor_copy(rbs[:, :], pC[0:128, 0:CT]), r=RC, w=("rbs",))

    def ln_apply(src_t, src_res, dst_t, dst_res, resid_t=None, resid_res=None):
        for b in range(8):
            scr2 = sqscr2
            O("vector", (lambda e, b=b: e.tensor_sub(scr2[:, :], blk(src_t, b), mbs[:, :])),
              r=((src_res, b), "mbs"), w=("sqscr2",))
            if resid_t is None:
                O("vector", (lambda e, b=b: e.tensor_mul(blk(dst_t, b), scr2[:, :], rbs[:, :])),
                  r=("sqscr2", "rbs"), w=((dst_res, b),))
            else:
                O("vector", (lambda e, b=b: e.tensor_mul(scr2[:, :], scr2[:, :], rbs[:, :])),
                  r=("sqscr2", "rbs"), w=("sqscr2",))
                O("vector", (lambda e, b=b: e.tensor_add(blk(dst_t, b), scr2[:, :],
                                                         blk(resid_t, b))),
                  r=("sqscr2", (resid_res, b)), w=((dst_res, b),))

    wsl = [0]

    def stream_mm(wS, din, dout, pmaps, mov_ap, mov_res, epi):
        nob = dout // 128
        nsb = din // 512
        G = len(pmaps)
        for og0 in range(0, nob, G):
            ogn = min(G, nob - og0)
            width = ogn * 128
            for sbk in range(nsb):
                slot = wslot[wsl[0] % 4]
                sres = f"wslot{wsl[0] % 4}"
                wsl[0] += 1
                O("sync", (lambda e, sbk=sbk, og0=og0, slot=slot, width=width, wS=wS:
                           e.dma_start(slot[:, 0:4 * width].rearrange("p (jj o) -> p jj o", jj=4),
                                       scr[wS][512 * sbk:512 * sbk + 512,
                                               128 * og0:128 * og0 + width]
                                       .rearrange("(jj p) o -> p jj o", p=128))),
                  r=tuple((wS, 4 * sbk + j) for j in range(4)), w=(sres,), dma=True, fam=sres)
                for jj in range(4):
                    j = 4 * sbk + jj
                    for oi in range(ogn):
                        pap, bres = pmaps[oi]
                        stat = slot[:, jj * width + oi * 128: jj * width + (oi + 1) * 128]
                        O("tensor", (lambda e, pap=pap, stat=stat, mv=mov_ap(j),
                                     first=(sbk == 0 and jj == 0),
                                     last=(sbk == nsb - 1 and jj == 3):
                                     e.matmul(pap, stat, mv, start=first, stop=last)),
                          r=(sres, mov_res(j)), w=bres)
            for oi in range(ogn):
                epi(og0 + oi, *pmaps[oi])

    def pmap4():
        return [(pA[:, 0:CT], ("pA0",)), (pA[:, 512:512 + CT], ("pA1",)),
                (pB[:, 0:CT], ("pB0",)), (pB[:, 512:512 + CT], ("pB1",))]

    def logsig(psrc, psrc_res, bias, dst, dst_res):
        (zap, zr), (sap, sr), (map_, mr) = r2[0], r2[1], r2[2]
        O("vector", (lambda e: e.tensor_copy(zap, psrc)),
          r=(psrc_res,), w=(zr,))
        O("vector", (lambda e: e.tensor_add(zap, zap, S[bias][:, :])),
          r=(zr, bias), w=(zr,))
        O("scalar", lambda e: e.activation(sap, zap, AF.Abs),
          r=(zr,), w=(sr,))
        O("scalar", lambda e: e.activation(sap, sap, AF.Exp, scale=-1.0),
          r=(sr,), w=(sr,))
        O("scalar", lambda e: e.activation(sap, sap, AF.Ln, bias=1.0),
          r=(sr,), w=(sr,))
        O("vector", lambda e: e.tensor_single_scalar(map_, zap, 0.0, ALU.min),
          r=(zr,), w=(mr,))
        O("vector", (lambda e: e.tensor_sub(dst, map_, sap)),
          r=(mr, sr), w=(dst_res,))

    seg = CT // 64

    for ti in range(NTILE):
        ts = ti * CT
        O("gpsimd", (lambda e, ts=ts: e.dma_start(
            xbf[:, :].rearrange("p (b t) -> p b t", b=8),
            xt[:, ts:ts + CT].rearrange("(b p) t -> p b t", p=128))),
          w=tuple(("xbf", b) for b in range(8)), dma=True, fam="xin")

        ln_stats(xbf, "xbf")
        ln_apply(xbf, "xbf", xn, "xn")

        def epi_copy(dst_t, dres):
            def f(o, pap, bres):
                O("scalar", (lambda e, o=o, pap=pap: e.copy(blk(dst_t, o), pap)),
                  r=bres, w=((dres, o),))
            return f

        def epi_silu(dst_t, dres):
            def f(o, pap, bres):
                O("scalar", (lambda e, o=o, pap=pap: e.activation(blk(dst_t, o), pap, AF.Silu)),
                  r=bres, w=((dres, o),))
            return f

        mov_xn = lambda j: blk(xn, j)
        res_xn = lambda j: ("xn", j)
        stream_mm("wqS", DM, DM, pmap4(), mov_xn, res_xn, epi_copy(qg, "qg"))
        stream_mm("wkS", DM, DM, pmap4(), mov_xn, res_xn, epi_copy(kn, "kn"))
        stream_mm("wvS", DM, DM, pmap4(), mov_xn, res_xn, epi_copy(vb, "vb"))
        stream_mm("wgS", DM, DM, pmap4(), mov_xn, res_xn, epi_silu(gate, "gate"))

        for j in range(8):
            O("tensor", (lambda e, j=j: e.matmul(pA[0:16, 0:CT], wabsb[:, j * 32:j * 32 + 16],
                                                 blk(xn, j), start=(j == 0), stop=(j == 7))),
              r=(("xn", j), "wabsb"), w=("pA0",))
            O("tensor", (lambda e, j=j: e.matmul(pB[0:16, 0:CT],
                                                 wabsb[:, j * 32 + 16:j * 32 + 32],
                                                 blk(xn, j), start=(j == 0), stop=(j == 7))),
              r=(("xn", j), "wabsb"), w=("pB0",))

        logsig(pA[0:16, 0:CT], "pA0", "bacol", rla[0], rla[1])
        O("vector", lambda e: e.tensor_single_scalar(rla[0], rla[0], -60.0, ALU.max),
          r=(rla[1],), w=(rla[1],))
        logsig(pB[0:16, 0:CT], "pB0", "bbcol", r2[3][0], r2[3][1])
        O("scalar", lambda e: e.activation(rbr[0], r2[3][0], AF.Exp),
          r=(r2[3][1],), w=(rbr[1],))
        O("vector", lambda e: e.tensor_copy(rG[0], rla[0]),
          r=(rla[1],), w=(rG[1],))
        for sh in (1, 2, 4, 8, 16, 32):
            O("vector", (lambda e, sh=sh: e.tensor_add(
                rgp[0].rearrange("p (s c) -> p s c", c=64)[:, :, sh:64],
                rG[0].rearrange("p (s c) -> p s c", c=64)[:, :, sh:64],
                rG[0].rearrange("p (s c) -> p s c", c=64)[:, :, 0:64 - sh])),
              r=(rG[1],), w=(rgp[1],))
            O("vector", (lambda e, sh=sh: e.tensor_copy(
                rgp[0].rearrange("p (s c) -> p s c", c=64)[:, :, 0:sh],
                rG[0].rearrange("p (s c) -> p s c", c=64)[:, :, 0:sh])),
              r=(rG[1],), w=(rgp[1],))
            O("vector", lambda e: e.tensor_copy(rG[0], rgp[0]),
              r=(rgp[1],), w=(rG[1],))
        O("scalar", lambda e: e.activation(rgv[0], rG[0], AF.Exp),
          r=(rG[1],), w=(rgv[1],))
        gend = rG[0].rearrange("p (s c) -> p s c", c=64)[:, :, 63:64] \
            .broadcast_to([16, seg, 64])
        O("vector", lambda e: e.tensor_sub(
            rgr[0].rearrange("p (s c) -> p s c", c=64),
            rG[0].rearrange("p (s c) -> p s c", c=64), gend),
          r=(rG[1],), w=(rgr[1],))
        O("scalar", lambda e: e.activation(rgr[0], rgr[0], AF.Exp, scale=-1.0),
          r=(rgr[1],), w=(rgr[1],))
        O("vector", lambda e: e.tensor_copy(rowg[:, 0:CT], rG[0]),
          r=(rG[1],), w=("Ghib",))
        O("vector", lambda e: e.tensor_sub(rowg[:, CT:2 * CT], rG[0],
                                           rowg[:, 0:CT]),
          r=(rG[1], "Ghib"), w=("Glob",))
        gvend = rgv[0].rearrange("p (s c) -> p s c", c=64)[:, :, 63:64] \
            .broadcast_to([16, seg, 8])
        O("vector", (lambda e: e.tensor_mul(
            mov2[0][:, 0:8 * seg].rearrange("p (s c) -> p s c", c=8), gvend,
            K["c_pairrep"][:, 0:8 * seg].rearrange("p (s c) -> p s c", c=8))),
          r=(rgv[1], "c_pairrep"), w=(("m2", 0),))
        O("tensor", lambda e: e.matmul(pS[0:128, 0:8 * seg], K["c_parity"][:, :],
                                       mov2[0][:, 0:8 * seg], start=True, stop=True),
          r=(("m2", 0), "c_parity"), w=("pS",))
        O("vector", lambda e: e.tensor_copy(gC[:, 0:8 * seg], pS[0:128, 0:8 * seg]),
          r=("pS",), w=("gC",))

        for raw_t, raw_res, pap, bres, dst, dres in (
                (qg, "qg", pS[0:16, 0:CT], ("pS",), r2[0][0], r2[0][1]),
                (kn, "kn", pC[0:16, 0:CT], RC, r2[1][0], r2[1][1])):
            for b in range(8):
                sq = (sqscr, sqscr2)[b % 2]
                sqr = ("sqscr", "sqscr2")[b % 2]
                O("vector", (lambda e, b=b, raw_t=raw_t, sq=sq: e.tensor_mul(
                    sq[:, :], blk(raw_t, b), blk(raw_t, b))),
                  r=((raw_res, b),), w=(sqr,))
                O("tensor", (lambda e, b=b, pap=pap, sq=sq: e.matmul(
                    pap, K["c_sq16"][:, b * 16:b * 16 + 16], sq[:, :],
                    start=(b == 0), stop=(b == 7))),
                  r=(sqr, "c_sq16"), w=bres)
            O("scalar", (lambda e, pap=pap, dst=dst: e.activation(dst, pap, AF.Ln)),
              r=bres, w=(dres,))
            O("scalar", (lambda e, dst=dst: e.activation(dst, dst, AF.Exp, scale=-0.5)),
              r=(dres,), w=(dres,))
            O("vector", (lambda e, dst=dst: e.tensor_single_scalar(dst, dst, 1e6, ALU.min)),
              r=(dres,), w=(dres,))
        O("vector", lambda e: e.tensor_mul(r2[2][0], r2[0][0], rgv[0]),
          r=(r2[0][1], rgv[1]), w=(r2[2][1],))
        O("vector", lambda e: e.scalar_tensor_tensor(r2[3][0], r2[1][0],
                                                     -1.0, rbr[0],
                                                     ALU.mult, ALU.mult),
          r=(r2[1][1], rbr[1]), w=(r2[3][1],))
        O("vector", lambda e: e.tensor_mul(r2[4][0], r2[3][0], rgv[0]),
          r=(r2[3][1], rgv[1]), w=(r2[4][1],))
        rowsrc = [(r2[0][1], r2[0][0], rowbA[:, 0:CT]),
                  (r2[2][1], r2[2][0], rowbA[:, CT:2 * CT]),
                  (r2[1][1], r2[1][0], rowbA[:, 2 * CT:3 * CT]),
                  (r2[3][1], r2[3][0], rowbA[:, 3 * CT:4 * CT]),
                  (r2[4][1], r2[4][0], rowbB[:, 0:CT]),
                  (rbr[1], rbr[0], rowbB[:, CT:2 * CT])]
        for res, src, dst in rowsrc:
            O("vector", (lambda e, src=src, dst=dst: e.tensor_copy(dst, src)),
              r=(res,), w=((res, "bf"),))
        rowap = {}
        rowres = {}
        for nm, (res, _, dst) in zip(("rnq", "rg", "rnk", "rkb", "rkg", "brow"), rowsrc):
            rowap[nm] = dst
            rowres[nm] = (res, "bf")

        for b in range(8):
            def bc(res, pap, bres):
                O("tensor", (lambda e, res=res, pap=pap, b=b: e.matmul(
                    pap, K["c_b16"][:, b * 128:(b + 1) * 128], rowap[res],
                    start=True, stop=True)),
                  r=(rowres[res], "c_b16"), w=bres)
            bc("rnq", pS[0:128, 0:CT], ("pS",))
            O("vector", (lambda e, b=b: e.scalar_tensor_tensor(
                blk(qn, b), pS[0:128, 0:CT], 1.0, blk(qg, b), ALU.mult, ALU.mult)),
              r=("pS", ("qg", b)), w=(("qn", b),))
            bc("rg", pC[0:128, 0:CT], RC)
            O("vector", (lambda e, b=b: e.scalar_tensor_tensor(
                blk(qg, b), pC[0:128, 0:CT], 1.0, blk(qg, b), ALU.mult, ALU.mult)),
              r=RC + (("qg", b),), w=(("qg", b),))
            bc("rkb", pS[0:128, 0:CT], ("pS",))
            O("vector", (lambda e, b=b: e.scalar_tensor_tensor(
                blk(kb, b), pS[0:128, 0:CT], 1.0, blk(kn, b), ALU.mult, ALU.mult)),
              r=("pS", ("kn", b)), w=(("kb", b),))
            bc("rkg", pC[0:128, 0:CT], RC)
            O("vector", (lambda e, b=b: e.scalar_tensor_tensor(
                blk(kgb, b), pC[0:128, 0:CT], 1.0, blk(kn, b), ALU.mult, ALU.mult)),
              r=RC + (("kn", b),), w=(("kgb", b),))
            bc("rnk", pS[0:128, 0:CT], ("pS",))
            O("vector", (lambda e, b=b: e.scalar_tensor_tensor(
                blk(kn, b), pS[0:128, 0:CT], 1.0, blk(kn, b), ALU.mult, ALU.mult)),
              r=("pS", ("kn", b)), w=(("kn", b),))
            bc("brow", pC[0:128, 0:CT], RC)
            O("vector", (lambda e, b=b: e.scalar_tensor_tensor(
                blk(vb, b), pC[0:128, 0:CT], 1.0, blk(vb, b), ALU.mult, ALU.mult)),
              r=RC + (("vb", b),), w=(("vb", b),))

        for d in range(NDBL if "delta" not in SKIP else 0):
            t0 = d * 128

            def hsl(t8, h, c):
                return blk(t8, h // 2)[64 * (h % 2):64 * (h % 2) + 64,
                                       t0 + 64 * c: t0 + 64 * c + 64]

            O("tensor", (lambda e, t0=t0: e.transpose(
                pS[0:128, 0:16], rowf1[:, 2 * CT + t0:2 * CT + t0 + 128],
                K["c_id32"][0:16, 0:16])),
              r=(rgr[1], "c_id32"), w=("pS",))
            O("vector", lambda e: e.tensor_copy(SMTb[:, 0:16], pS[0:128, 0:16]),
              r=("pS",), w=("SMTb",))

            for g2 in range(2 if "dpm" not in SKIP and "dem" not in SKIP else 0):
                for rres, rap in (("Ghib", rowg[:, t0:t0 + 128]),
                                  ("Glob", rowg[:, CT + t0:CT + t0 + 128])):
                    O("tensor", (lambda e, g2=g2, rap=rap, first=(rres == "Ghib"):
                                 e.matmul(pB[:, 512 * g2:512 * g2 + 512], rap,
                                          K["c_ind64"][:, 512 * g2:512 * g2 + 512],
                                          start=first, stop=False)),
                      r=(rres, "c_ind64"), w=(RB[g2],))
            for half, nmsk in ((0, "c_nonel"), (1, "c_noneh")):
                for hl, rres in ((0, "Ghib"), (1, "Glob")):
                    c0r = (0, CT)[hl]
                    src = rowg[:, c0r + t0 + 64 * half: c0r + t0 + 64 * half + 64] \
                        .unsqueeze(1).broadcast_to([16, 16, 64])
                    O("vector", (lambda e, hl=hl, src=src: e.tensor_mul(
                        mov2[hl][:, :].rearrange("p (h c) -> p h c", c=64),
                        K["c_ind64"][:, :].rearrange("p (h c) -> p h c", c=64), src)),
                      r=(rres, "c_ind64"), w=(("m2", hl),))
                for g2 in range(2):
                    for hl in (0, 1):
                        last = (half == 1 and hl == 1)
                        O("tensor", (lambda e, g2=g2, hl=hl, nmsk=nmsk, last=last:
                                     e.matmul(pB[:, 512 * g2:512 * g2 + 512], K[nmsk][:, :],
                                              mov2[hl][:, 512 * g2:512 * g2 + 512],
                                              start=False, stop=last)),
                          r=(("m2", hl), nmsk), w=(RB[g2],))
            O("vector", lambda e: e.tensor_single_scalar(escr[:, :], pB[:, :], 0.0, ALU.min),
              r=RB, w=("escr",))
            O("scalar", lambda e: e.activation(Eun[:, :], escr[:, :], AF.Exp),
              r=("escr",), w=("EunR",))
            O("vector", lambda e: e.tensor_scalar(escr[:, :], pB[:, :], -1.0, 0.0,
                                                  ALU.mult, ALU.min),
              r=RB, w=("escr",))
            O("scalar", lambda e: e.activation(Eprs[:, :], escr[:, :], AF.Exp),
              r=("escr",), w=("EprR",))
            O("vector", lambda e: e.tensor_mul(
                Epri[:, :].rearrange("p (h c) -> p h c", c=64),
                Eprs[:, :].rearrange("p (h c) -> p h c", c=64),
                K["c_maskpri"][:, :].unsqueeze(1).broadcast_to([128, 16, 64])),
              r=("EprR", "c_maskpri"), w=("Epri",))
            O("vector", lambda e: e.tensor_mul(
                Eprs[:, :].rearrange("p (h c) -> p h c", c=64),
                Eprs[:, :].rearrange("p (h c) -> p h c", c=64),
                K["c_maskpr"][:, :].unsqueeze(1).broadcast_to([128, 16, 64])),
              r=("EprR", "c_maskpr"), w=("Eprs",))
            O("vector", lambda e: e.tensor_mul(
                Eun[:, :].rearrange("p (h c) -> p h c", c=64),
                Eun[:, :].rearrange("p (h c) -> p h c", c=64),
                K["c_maskun"][:, :].unsqueeze(1).broadcast_to([128, 16, 64])),
              r=("EunR", "c_maskun"), w=("Eun",))

            ablocks = [(0, 0), (0, 64), (64, 64), (64, 0)]
            if "dem" in SKIP:
                ablocks = []
            for bi, (hb, cb) in enumerate(ablocks):
                c = cb // 64
                first = True
                for h in range(hb // 64, H, 2):
                    tp = (hb, cb) if (hb or cb) else None
                    bres_h = lambda pair: (pair[0],) if h < 8 else (pair[1],)
                    ps = (bi == 2) and first
                    first = False
                    O("tensor", (lambda e, s=hsl(kb, h, c), m=hsl(kn, h, c), h=h, cb=cb, tp=tp:
                                 e.matmul(pA[cb:cb + 64, 64 * h:64 * h + 64], s, m,
                                          start=True, stop=True, tile_position=tp)),
                      r=(("kb", h // 2), ("kn", h // 2)), w=bres_h(RA), pe_sync=ps)
                    O("tensor", (lambda e, s=hsl(kn, h, c), m=hsl(kb, h, c), h=h, cb=cb, tp=tp:
                                 e.matmul(pC[cb:cb + 64, 64 * h:64 * h + 64], s, m,
                                          start=True, stop=True, tile_position=tp)),
                      r=(("kb", h // 2), ("kn", h // 2)), w=bres_h(RC))
                    O("tensor", (lambda e, s=hsl(kn, h, c), m=hsl(qn, h, c), h=h, cb=cb, tp=tp:
                                 e.matmul(pB[cb:cb + 64, 64 * h:64 * h + 64], s, m,
                                          start=True, stop=True, tile_position=tp)),
                      r=(("kn", h // 2), ("qn", h // 2)), w=bres_h(RB))
            O("vector", lambda e: e.tensor_copy(Aun_s[:, :], pA[:, :]), r=RA, w=("Aun_s",))
            O("scalar", lambda e: e.copy(Aprs[:, :], pC[:, :]), r=RC, w=("Aprs",))
            O("vector", lambda e: e.tensor_copy(Akqs[:, :], pB[:, :]), r=RB, w=("Akqs",))
            O("vector", lambda e: e.tensor_mul(Xp[1][:, :], Aprs[:, :], Eprs[:, :]),
              r=("Aprs", "Eprs"), w=("Xp1",))
            O("vector", lambda e: e.tensor_mul(Xua[:, :], Aun_s[:, :], Eun[:, :]),
              r=("Aun_s", "Eun"), w=("Xua",))
            O("vector", lambda e: e.tensor_mul(P0s[:, :], Akqs[:, :], Epri[:, :]),
              r=("Akqs", "Epri"), w=("P0s",))

            for src_t, sres, dst, dres in (((vb, "vb", VbTs, "VbTs"),
                                            (kn, "kn", Ktoks, "Ktoks"))
                                           if "dtp" not in SKIP else ()):
                for pr in range(8):
                    for c in range(2):
                        cb = 64 * c
                        src = blk(src_t, pr)[:, t0 + 64 * c: t0 + 64 * c + 64]
                        O("tensor", (lambda e, src=src, pr=pr, cb=cb: e.transpose(
                            ptr[cb:cb + 64, 128 * pr:128 * pr + 128], src,
                            K["c_idbf"][:, :],
                            tile_position=(0, cb) if cb else None)),
                          r=((sres, pr), "c_idbf"), w=("ptr",))
                O("vector", (lambda e, dst=dst: e.tensor_copy(dst[:, :], ptr[:, :])),
                  r=("ptr",), w=(dres,))

            prevp, prevu = "Xp1", "Xua"
            prevpt, prevut = Xp[1], Xua
            for m in ((2, 4, 8, 16, 32) if "dsq" not in SKIP else ()):
                for h in range(H):
                    for c in range(2):
                        cb = 64 * c
                        tp = (cb, cb) if cb else None
                        O("tensor", (lambda e, h=h, cb=cb, tp=tp, su=prevut, sp=prevpt:
                                     e.matmul(pA[cb:cb + 64, 64 * h:64 * h + 64],
                                              su[cb:cb + 64, 64 * h:64 * h + 64],
                                              sp[cb:cb + 64, 64 * h:64 * h + 64],
                                              start=True, stop=True, tile_position=tp)),
                          r=(prevu, prevp), w=(RA[0],) if h < 8 else (RA[1],))
                        if m != 32:
                            O("tensor", (lambda e, h=h, cb=cb, tp=tp, su=prevut, sp=prevpt:
                                         e.matmul(pC[cb:cb + 64, 64 * h:64 * h + 64],
                                                  sp[cb:cb + 64, 64 * h:64 * h + 64],
                                                  su[cb:cb + 64, 64 * h:64 * h + 64],
                                                  start=True, stop=True, tile_position=tp)),
                              r=(prevu, prevp), w=(RC[0],) if h < 8 else (RC[1],))
                O("vector", (lambda e, m=m: e.tensor_copy(Xp[m][:, :], pA[:, :])),
                  r=RA, w=(f"Xp{m}",))
                prevp, prevpt = f"Xp{m}", Xp[m]
                if m != 32:
                    nxt = Xub if prevut is Xua else Xua
                    nxtr = "Xub" if prevut is Xua else "Xua"
                    O("scalar", (lambda e, nxt=nxt: e.copy(nxt[:, :], pC[:, :])),
                      r=RC, w=(nxtr,))
                    prevu, prevut = nxtr, nxt

            for c in range(2 if "dzos" not in SKIP else 0):
                cb = 64 * c
                cg = 2 * d + c
                for g2 in range(2):
                    O("tensor", (lambda e, cb=cb, g2=g2: e.matmul(
                        pB[cb:cb + 64, 512 * g2:512 * g2 + 512],
                        K["c_idbf"][cb:cb + 64, cb:cb + 64],
                        VbTs[cb:cb + 64, 512 * g2:512 * g2 + 512],
                        start=True, stop=False, skip_group_check=True,
                        tile_position=(cb, cb) if cb else None)),
                      r=("VbTs", "c_idbf"), w=(RB[g2],))
                for par in (0, 1):
                    first = True
                    for h in range(par, H, 2):
                        hb = 64 * par
                        pr = h // 2
                        ps = first
                        first = False
                        O("tensor", (lambda e, s=hsl(kgb, h, c), hb=hb, pr=pr, cb=cb, h=h:
                                     e.matmul(pB[cb:cb + 64, 64 * h:64 * h + 64], s,
                                              STbf[hb:hb + 64, 64 * pr:64 * pr + 64],
                                              start=False, stop=False, skip_group_check=True,
                                              tile_position=(hb, cb) if (hb or cb) else None)),
                          r=(("kgb", h // 2), "STbf"), w=(RB[0],) if h < 8 else (RB[1],),
                          pe_sync=ps)
                O("vector", (lambda e, cb=cb: e.tensor_copy(Zs[0][cb:cb + 64, :],
                                                            pB[cb:cb + 64, :])),
                  r=RB, w=("Zs0",))
                zi = 0
                for m in (1, 2, 4, 8, 16, 32):
                    mt = Xp[m]
                    for h in range(H):
                        O("tensor", (lambda e, mt=mt, h=h, cb=cb, zi=zi, m=m:
                                     e.matmul(pB[cb:cb + 64, 64 * h:64 * h + 64],
                                              mt[cb:cb + 64, 64 * h:64 * h + 64],
                                              Zs[zi][cb:cb + 64, 64 * h:64 * h + 64],
                                              start=False, stop=False, skip_group_check=True,
                                              tile_position=(cb, cb) if cb else None)),
                          r=(f"Xp{m}", f"Zs{zi}"), w=(RB[0],) if h < 8 else (RB[1],))
                    zi = 1 - zi
                    O("vector", (lambda e, cb=cb, zi=zi: e.tensor_copy(
                        Zs[zi][cb:cb + 64, :], pB[cb:cb + 64, :])),
                      r=RB, w=(f"Zs{zi}",))
                O("vector", (lambda e, cb=cb, zi=zi: e.tensor_mul(
                    Wgt[cb:cb + 64, :].rearrange("p (h i) -> p h i", i=64),
                    Zs[zi][cb:cb + 64, :].rearrange("p (h i) -> p h i", i=64),
                    SMTb[cb:cb + 64, 0:16].unsqueeze(2).broadcast_to([64, 16, 64]))),
                  r=(f"Zs{zi}", "SMTb"), w=("Wgt",))
                for h in range(H):
                    hb = 64 * (h % 2)
                    pr = h // 2
                    col = 512 * c + 64 * pr
                    O("tensor", (lambda e, hb=hb, pr=pr, col=col, m=hsl(qg, h, c), h=h:
                                 e.matmul(pC[hb:hb + 64, col:col + 64],
                                          STbf[hb:hb + 64, 64 * pr:64 * pr + 64], m,
                                          start=(h <= 1), stop=False, skip_group_check=True,
                                          tile_position=(hb, hb) if hb else None)),
                      r=("STbf", ("qg", h // 2)), w=(RC[c],))
                for h in range(H):
                    hb = 64 * (h % 2)
                    pr = h // 2
                    col = 512 * c + 64 * pr
                    O("tensor", (lambda e, hb=hb, cb=cb, col=col, zi=zi, h=h:
                                 e.matmul(pC[hb:hb + 64, col:col + 64],
                                          Zs[zi][cb:cb + 64, 64 * h:64 * h + 64],
                                          P0s[cb:cb + 64, 64 * h:64 * h + 64],
                                          start=False, stop=(h >= H - 2), skip_group_check=True,
                                          tile_position=(cb, hb) if (cb or hb) else None)),
                      r=(f"Zs{zi}", "P0s"), w=(RC[c],))
                for h in range(H):
                    hb = 64 * (h % 2)
                    pr = h // 2
                    O("tensor", (lambda e, hb=hb, pr=pr, cb=cb, h=h: e.matmul(
                        pS[hb:hb + 64, 64 * pr:64 * pr + 64],
                        Ktoks[cb:cb + 64, 128 * pr + hb:128 * pr + hb + 64],
                        Wgt[cb:cb + 64, 64 * h:64 * h + 64],
                        start=(h <= 1), stop=(h >= H - 2), skip_group_check=True,
                        tile_position=(cb, hb) if (cb or hb) else None)),
                      r=("Ktoks", "Wgt"), w=("pS",))
                O("vector", (lambda e, cg=cg: e.tensor_mul(
                    stsc[:, :].rearrange("p (r i) -> p r i", i=64),
                    ST[:, :].rearrange("p (r i) -> p r i", i=64),
                    gC[:, cg * 8:cg * 8 + 8].unsqueeze(2).broadcast_to([128, 8, 64]))),
                  r=("ST", "gC"), w=("stsc",))
                O("vector", lambda e: e.tensor_add(ST[:, :], stsc[:, :], pS[0:128, 0:512]),
                  r=("stsc", "pS"), w=("ST",))
                O("vector", lambda e: e.tensor_copy(STbf[:, :], ST[:, :]),
                  r=("ST",), w=("STbf",))
            for c in range(2):
                ogap = gate[:, :].rearrange("p (pr tt) -> p pr tt", pr=8)[
                    :, :, t0 + 64 * c: t0 + 64 * c + 64]
                pcv = pC[:, 512 * c:512 * c + 512].rearrange("p (pr t) -> p pr t", pr=8)
                O("vector", (lambda e, ogap=ogap, pcv=pcv: e.scalar_tensor_tensor(
                    ogap, pcv, 1.0, ogap, ALU.mult, ALU.mult)),
                  r=(RC[c],) + tuple(("gate", b) for b in range(8)),
                  w=tuple(("gate", b) for b in range(8)))

        stream_mm("woS", DM, DM, pmap4(), lambda j: blk(gate, j), lambda j: ("gate", j),
                  epi_copy(vb, "vb"))
        ln_stats(vb, "vb")
        ln_apply(vb, "vb", xn, "xn", resid_t=xbf, resid_res="xbf")

        ln_stats(xn, "xn")
        ln_apply(xn, "xn", qn, "qn")

        mov_yn = lambda j: blk(qn, j)
        res_yn = lambda j: ("qn", j)
        nob = DF // 128
        for og0 in range(0, nob, 2):
            for wS, pmaps in (("w1S", [(pA[:, 0:CT], ("pA0",)), (pA[:, 512:512 + CT], ("pA1",))]),
                              ("w2S", [(pC[:, 0:CT], ("pC0",)), (pC[:, 512:512 + CT], ("pC1",))])):
                slot = wslot[wsl[0] % 4]
                sres = f"wslot{wsl[0] % 4}"
                wsl[0] += 1
                O("sync", (lambda e, og0=og0, slot=slot, wS=wS:
                           e.dma_start(slot[:, 0:2048].rearrange("p (jj o) -> p jj o", jj=8),
                                       scr[wS][0:1024, 128 * og0:128 * og0 + 256]
                                       .rearrange("(jj p) o -> p jj o", p=128))),
                  r=tuple((wS, j) for j in range(8)), w=(sres,), dma=True, fam=sres)
                for j in range(8):
                    for oi in range(2):
                        pap, bres = pmaps[oi]
                        stat = slot[:, j * 256 + oi * 128: j * 256 + (oi + 1) * 128]
                        O("tensor", (lambda e, pap=pap, stat=stat, mv=mov_yn(j),
                                     first=(j == 0), last=(j == 7):
                                     e.matmul(pap, stat, mv, start=first, stop=last)),
                          r=(sres, res_yn(j)), w=bres)
            for oi in range(2):
                o = og0 + oi
                papA = (pA[:, 0:CT], pA[:, 512:512 + CT])[oi]
                papC = (pC[:, 0:CT], pC[:, 512:512 + CT])[oi]
                brA = (("pA0",), ("pA1",))[oi]
                brC = (("pC0",), ("pC1",))[oi]
                uap, ures = u1map[o]
                O("scalar", (lambda e, o=o, papA=papA: e.activation(
                    silscr[:, :], papA, AF.Silu, scale=S["s1s"][:, o:o + 1])),
                  r=brA + ("s1s",), w=("silscr",))
                O("vector", (lambda e, o=o, papC=papC, uap=uap: e.scalar_tensor_tensor(
                    uap, papC, S["s2s"][:, o:o + 1], silscr[:, :], ALU.mult, ALU.mult)),
                  r=brC + ("s2s", "silscr"), w=(ures,))

        def epi_gelu(dmap, scal):
            def f(o, pap, bres):
                dap, dres = dmap[o]
                O("scalar", (lambda e, o=o, pap=pap, dap=dap: e.activation(
                    dap, pap, AF.Gelu, scale=S[scal][:, o:o + 1])),
                  r=bres + (scal,), w=(dres,))
            return f

        stream_mm("h1S", DF, DF, pmap4(), lambda j: u1map[j][0], lambda j: u1map[j][1],
                  epi_gelu(h2map, "hs1"))
        stream_mm("h2S", DF, DF, pmap4(), lambda j: h2map[j][0], lambda j: h2map[j][1],
                  epi_gelu(u1map, "hs2"))

        def epi_w3(o, pap, bres):
            O("scalar", (lambda e, o=o, pap=pap: e.mul(blk(qg, o), pap, S["w3s"][:, o:o + 1])),
              r=bres + ("w3s",), w=(("qg", o),))

        stream_mm("w3S", DF, DM, pmap4(), lambda j: u1map[j][0], lambda j: u1map[j][1],
                  epi_w3)

        ln_stats(qg, "qg")
        for b in range(8):
            ob = outb[b % 2]
            obres = f"outb{b % 2}"
            O("vector", (lambda e, b=b: e.tensor_sub(sqscr2[:, :], blk(qg, b), mbs[:, :])),
              r=(("qg", b), "mbs"), w=("sqscr2",))
            O("vector", (lambda e, b=b: e.tensor_mul(sqscr2[:, :], sqscr2[:, :], rbs[:, :])),
              r=("sqscr2", "rbs"), w=("sqscr2",))
            O("vector", (lambda e, b=b, ob=ob: e.tensor_add(ob[:, :], sqscr2[:, :],
                                                            blk(xn, b))),
              r=("sqscr2", ("xn", b)), w=(obres,))
            O("sync", (lambda e, b=b, ob=ob, ts=ts: e.dma_start(
                yt[128 * b:128 * b + 128, ts:ts + CT], ob[:, :])),
              r=(obres,), w=(("ydone", b % 2),), dma=True, fam=f"yo{b % 2}")

    O("sync", lambda e: e.nop(), r=(("ydone", 0), ("ydone", 1)))

    with nc.Block() as block:
        em.finalize(nc, block)
    return nc


_cache = {}


def prep_inputs(inputs, CT=512):
    cc = host_constants()
    base = {}
    for nm, src in [("wqT", "L0_attn_Wq"), ("wkT", "L0_attn_Wk"), ("wvT", "L0_attn_Wv"),
                    ("wgT", "L0_attn_Wg"), ("woT", "L0_attn_Wo"),
                    ("w1T", "L0_ffn_w1_w"), ("w2T", "L0_ffn_w2_w"),
                    ("h1T", "L0_ffn_h1_w"), ("h2T", "L0_ffn_h2_w"),
                    ("w3T", "L0_ffn_w3_w")]:
        base[nm] = np.ascontiguousarray(np.asarray(inputs[src], np.float32).T)
    base["wab"] = np.ascontiguousarray(np.concatenate(
        [np.asarray(inputs["L0_attn_Wa"], np.float32).T,
         np.asarray(inputs["L0_attn_Wb"], np.float32).T], axis=1))
    for nm, src in [("s1s", "L0_ffn_w1_s"), ("s2s", "L0_ffn_w2_s"),
                    ("hs1", "L0_ffn_h1_s"), ("hs2", "L0_ffn_h2_s")]:
        base[nm] = np.ascontiguousarray(np.asarray(inputs[src], np.float32).reshape(32, 128).T)
    base["w3s"] = np.ascontiguousarray(np.asarray(inputs["L0_ffn_w3_s"], np.float32).reshape(8, 128).T)
    base["bacol"] = np.ascontiguousarray(np.broadcast_to(
        np.asarray(inputs["L0_attn_ba"], np.float32).reshape(16, 1), (16, CT)))
    base["bbcol"] = np.ascontiguousarray(np.broadcast_to(
        np.asarray(inputs["L0_attn_bb"], np.float32).reshape(16, 1), (16, CT)))
    for nm in BF_CONSTS + F32_CONSTS:
        base[nm] = cc[nm]
    return base


def kernel(**inputs):
    T, CT = 8192, 512
    if "prog" not in _cache:
        _cache["prog"] = build_program(T, CT)
    nc = _cache["prog"]
    base = prep_inputs(inputs)
    x = np.asarray(inputs["L0_attn_x"], np.float32)
    in_maps = []
    for c in range(NCORES):
        m = dict(base)
        m["xt"] = np.ascontiguousarray(x[c].T)
        in_maps.append(m)
    res = run_bass_kernel_spmd(nc, in_maps, core_ids=list(range(NCORES)))
    out = np.stack([np.asarray(r["yt"]).T for r in res.results], axis=0)
    return out.astype(np.float32)


# version 15
# speedup vs baseline: 1.1242x; 1.0481x over previous
import os
import numpy as np
import concourse.bass as bass
import concourse.mybir as mybir
from concourse.bass_utils import run_bass_kernel_spmd

F32 = mybir.dt.float32
BF16 = mybir.dt.bfloat16
AF = mybir.ActivationFunctionType
ALU = mybir.AluOpType

H = 16
DH = 64
DM = 1024
DF = 4096
NCORES = 8
EPS_LN = 1e-5


class Em:

    ENGINES = ("tensor", "vector", "scalar", "sync", "gpsimd")

    TOTAL_FAMS = ("c_gp", "c_sync")

    def __init__(self):
        self.ops = []
        self.res_w = {}
        self.res_r = {}
        self.last_pe = None
        self.forced = {}

    def op(self, eng, fn, r=(), w=(), dma=False, fam=None, pe_sync=False):
        idx = len(self.ops)
        assert (fam is not None) == dma
        deps = set()
        if pe_sync and self.last_pe is not None:
            deps.add(self.last_pe)
            self.forced[idx] = self.last_pe
        for res in r:
            lw = self.res_w.get(res)
            if lw is not None:
                deps.add(lw)
        for res in w:
            lw = self.res_w.get(res)
            if lw is not None:
                deps.add(lw)
            deps.update(self.res_r.get(res, {}).values())
        rkey = eng if fam is None else ("dma", fam)
        for res in r:
            self.res_r.setdefault(res, {})[rkey] = idx
        for res in w:
            self.res_w[res] = idx
            self.res_r[res] = {}
        if eng == "tensor" and fam is None:
            self.last_pe = idx
        self.ops.append((eng, fn, deps, fam))
        return idx

    def finalize(self, nc, block):
        ops = self.ops
        fam_ord = {}
        fam_cnt = {}
        for i, (eng, fn, deps, fam) in enumerate(ops):
            if fam is not None:
                fam_ord[i] = fam_cnt.get(fam, 0)
                fam_cnt[fam] = fam_ord[i] + 1
        marked = set()
        for i, (eng, fn, deps, fam) in enumerate(ops):
            for d in deps:
                deng, _, _, dfam = ops[d]
                if dfam is None:
                    if not (deng == eng == "tensor") or self.forced.get(i) == d:
                        marked.add(d)
        mark_ord = {}
        cnt = {e: 0 for e in self.ENGINES}
        per_eng = {e: [] for e in self.ENGINES}
        for i, (eng, fn, deps, fam) in enumerate(ops):
            per_eng[eng].append(i)
            if i in marked:
                cnt[eng] += 1
                mark_ord[i] = cnt[eng]
        self._sem_ctx = []
        sems = {}
        fsems = {}
        for e in self.ENGINES:
            cm = nc.semaphore(f"es_{e}")
            self._sem_ctx.append(cm)
            sems[e] = cm.__enter__()
        for fam in fam_cnt:
            cm = nc.semaphore(f"df_{fam}")
            self._sem_ctx.append(cm)
            fsems[fam] = cm.__enter__()

        def body_for(ename):
            def body(engine):
                wm = {}
                for i in per_eng[ename]:
                    eng, fn, deps, fam = ops[i]
                    need = {}
                    for d in deps:
                        deng, _, _, dfam = ops[d]
                        if dfam is not None:
                            key = ("d", dfam)
                            if dfam in self.TOTAL_FAMS:
                                val = 16 * fam_cnt[dfam]
                            else:
                                val = 16 * (fam_ord[d] + 1)
                        else:
                            if deng == ename == "tensor" and self.forced.get(i) != d:
                                continue
                            key = ("e", deng)
                            val = mark_ord[d]
                        if need.get(key, -1) < val:
                            need[key] = val
                    for key in sorted(need, key=str):
                        val = need[key]
                        if wm.get(key, -1) >= val:
                            continue
                        wm[key] = val
                        engine.wait_ge((fsems[key[1]] if key[0] == "d" else sems[key[1]]), val)
                    inst = fn(engine)
                    if fam is not None:
                        inst.then_inc(fsems[fam], 16)
                    elif i in marked:
                        inst.then_inc(sems[ename], 1)
            return body

        for ename in self.ENGINES:
            getattr(block, ename)(body_for(ename))


def host_constants():
    c = {}
    ar = np.arange(64)
    a128 = np.arange(128) % 64
    c["c_id32"] = np.eye(128, dtype=np.float32)
    c["c_idbf"] = np.eye(128, dtype=np.float32)
    c["c_maskun"] = (ar[None, :] < a128[:, None]) * np.float32(1)
    c["c_maskpr"] = (ar[None, :] > a128[:, None]) * np.float32(1)
    c["c_maskpri"] = (ar[None, :] >= a128[:, None]) * np.float32(1)
    ind = np.zeros((16, 16, 64), np.float32)
    for j in range(16):
        ind[j, j, :] = 1.0
    c["c_ind64"] = ind.reshape(16, 1024)
    c["c_nonel"] = np.tile(np.where(np.arange(128) < 64, -1.0, 0.0).astype(np.float32), (16, 1))
    c["c_noneh"] = np.tile(np.where(np.arange(128) >= 64, -1.0, 0.0).astype(np.float32), (16, 1))
    b16 = np.zeros((16, 8, 128), np.float32)
    for blk in range(8):
        b16[2 * blk, blk, 0:64] = 1.0
        b16[2 * blk + 1, blk, 64:128] = 1.0
    c["c_b16"] = b16.reshape(16, 1024)
    sq = np.zeros((128, 8, 16), np.float32)
    for blk in range(8):
        sq[0:64, blk, 2 * blk] = 1.0
        sq[64:128, blk, 2 * blk + 1] = 1.0
    c["c_sq16"] = sq.reshape(128, 128)
    par = np.zeros((16, 128), np.float32)
    for m in range(16):
        par[m, 0:64] = 1.0 if m % 2 == 0 else 0.0
        par[m, 64:128] = 0.0 if m % 2 == 0 else 1.0
    c["c_parity"] = par
    pr = np.zeros((16, 8, 8), np.float32)
    for m in range(16):
        pr[m, :, m // 2] = 1.0
    c["c_pairrep"] = pr.reshape(16, 64)
    c["c_onesrow"] = np.ones((1, 128), np.float32)
    c["c_onescol"] = np.ones((128, 1), np.float32)
    return c

BF_CONSTS = ["c_idbf", "c_maskun", "c_maskpr", "c_maskpri", "c_ind64",
             "c_nonel", "c_noneh", "c_b16", "c_sq16", "c_parity",
             "c_pairrep", "c_onescol"]
F32_CONSTS = ["c_id32", "c_onesrow"]

WEIGHTS = [("wqT", DM, DM), ("wkT", DM, DM), ("wvT", DM, DM), ("wgT", DM, DM),
           ("woT", DM, DM), ("w1T", DM, DF), ("w2T", DM, DF),
           ("h1T", DF, DF), ("h2T", DF, DF), ("w3T", DF, DM)]


SKIP = set(os.environ.get("KSKIP", "").split(",")) - {""}


def build_program(T=8192, CT=512):
    nc = bass.Bass("TRN2", target_bir_lowering=False)
    NTILE = T // CT
    NDBL = CT // 128
    cc = host_constants()

    xt = nc.dram_tensor("xt", [DM, T], F32, kind="ExternalInput").ap()
    yt = nc.dram_tensor("yt", [DM, T], F32, kind="ExternalOutput").ap()
    w_in = {nm: nc.dram_tensor(nm, [a, b], F32, kind="ExternalInput").ap()
            for nm, a, b in WEIGHTS}
    w_in["wab"] = nc.dram_tensor("wab", [DM, 32], F32, kind="ExternalInput").ap()
    sc_in = {nm: nc.dram_tensor(nm, list(shp), F32, kind="ExternalInput").ap()
             for nm, shp in [("s1s", (128, 32)), ("s2s", (128, 32)),
                             ("hs1", (128, 32)), ("hs2", (128, 32)),
                             ("w3s", (128, 8)), ("bacol", (16, CT)), ("bbcol", (16, CT))]}
    cst_in = {nm: nc.dram_tensor(nm, list(cc[nm].shape), F32, kind="ExternalInput").ap()
              for nm in BF_CONSTS + F32_CONSTS}
    scr = {nm.replace("T", "S"): nc.dram_tensor(nm.replace("T", "S"), [a, b], BF16).ap()
           for nm, a, b in WEIGHTS}

    em = Em()
    _ctx = []

    def sb(name, shape, dt):
        t = nc.sbuf_tensor(name, shape, dt)
        _ctx.append(t)
        return t.__enter__()

    def psum(name, shape, dt):
        t = nc.psum_tensor(name, shape, dt)
        _ctx.append(t)
        return t.__enter__()

    K = {}
    for nm in BF_CONSTS:
        K[nm] = sb("sb_" + nm, list(cc[nm].shape), BF16)
    for nm in F32_CONSTS:
        K[nm] = sb("sb_" + nm, list(cc[nm].shape), F32)
    S = {nm: sb("sb_" + nm, list(ap.shape), F32) for nm, ap in sc_in.items()}
    wabsb = sb("wabsb", [128, 256], BF16)
    wslot = [sb(f"wslot{i}", [128, 2048], BF16) for i in range(3)]
    ST = sb("ST", [128, 512], F32)
    STbf = sb("STbf", [128, 512], BF16)
    stsc = sb("stsc", [128, 512], F32)
    outb = [sb(f"outb{i}", [128, CT], F32) for i in range(2)]

    xbf = sb("xbf", [128, 8 * CT], BF16)
    xn = sb("xn", [128, 8 * CT], BF16)
    qn = sb("qn", [128, 8 * CT], BF16)
    qg = sb("qg", [128, 8 * CT], BF16)
    kn = sb("kn", [128, 8 * CT], BF16)
    kb = sb("kb", [128, 8 * CT], BF16)
    kgb = sb("kgb", [128, 8 * CT], BF16)
    vb = sb("vb", [128, 8 * CT], BF16)
    gate = sb("gate", [128, 8 * CT], BF16)

    sqscr = sb("sqscr", [128, CT], BF16)
    sqscr2 = sb("sqscr2", [128, CT], BF16)
    mbs = sb("mbs", [128, CT], BF16)
    rbs = sb("rbs", [128, CT], BF16)
    rowsA = sb("rowsA", [1, 2 * CT], F32)
    rowsB = sb("rowsB", [1, 2 * CT], F32)
    rowf1 = sb("rowf1", [16, 5 * CT], F32)
    rowf2 = sb("rowf2", [16, 5 * CT], F32)
    rG = (rowf1[:, 0:CT], ("rf1", 0))
    rgv = (rowf1[:, CT:2 * CT], ("rf1", 1))
    rgr = (rowf1[:, 2 * CT:3 * CT], ("rf1", 2))
    rbr = (rowf1[:, 3 * CT:4 * CT], ("rf1", 3))
    rla = (rowf1[:, 4 * CT:5 * CT], ("rf1", 4))
    rgp = rla
    r2 = [(rowf2[:, i * CT:(i + 1) * CT], ("rf2", i)) for i in range(5)]
    rowbA = sb("rowbA", [16, 4 * CT], BF16)
    rowbB = sb("rowbB", [16, 2 * CT], BF16)
    rowg = sb("rowg", [16, 2 * CT], BF16)
    mov2 = [sb("mov2_0", [16, 1024], BF16)]
    gC = sb("gC", [128, 64], F32)
    SMTb = sb("SMTb", [128, 16], BF16)
    epsb = sb("epsb", [1, 1], F32)

    escr = sb("escr", [128, 1024], F32)
    Eun = sb("Eun", [128, 1024], BF16)
    Eprs = sb("Eprs", [128, 1024], BF16)
    Epri = sb("Epri", [128, 1024], BF16)
    Aun_s = sb("Aun_s", [128, 1024], BF16)
    Aprs = sb("Aprs", [128, 1024], BF16)
    Akqs = sb("Akqs", [128, 1024], BF16)
    P0s = sb("P0s", [128, 1024], BF16)
    Xp = [{m: sb(f"Xp{m}_{i}", [128, 1024], BF16) for m in (1, 2, 4, 8, 16, 32)}
          for i in range(2)]
    Xua = sb("Xua", [128, 1024], BF16)
    Xub = sb("Xub", [128, 1024], BF16)
    Zs = [sb(f"Zs{i}", [128, 1024], BF16) for i in range(2)]
    Wgt = sb("Wgt", [128, 1024], BF16)
    VbTs = sb("VbTs", [128, 1024], BF16)
    Ktoks = sb("Ktoks", [128, 1024], BF16)
    P0s2 = [sb(f"P0s{i}", [128, 1024], BF16) for i in range(2)]
    SMTb2 = [sb(f"SMTb{i}", [128, 16], BF16) for i in range(2)]

    pA = psum("pA", [128, 1024], F32)
    pB = psum("pB", [128, 1024], F32)
    pC = psum("pC", [128, 1024], F32)
    pS = psum("pS", [128, 512], F32)
    ptr = psum("ptr", [128, 1024], BF16)
    RA = ("pA0", "pA1")
    RB = ("pB0", "pB1")
    RC = ("pC0", "pC1")

    O = em.op

    def blk(t8, b):
        return t8[:, b * CT:(b + 1) * CT]

    def mk_map(pairs):
        out = []
        for t8, rname in pairs:
            nb = t8.shape[1] // CT
            for b in range(nb):
                res = (rname, b) if rname in ("kn", "kb", "kgb", "vb", "gate", "xbf") else rname
                out.append((t8[:, b * CT:(b + 1) * CT], res))
        return out

    u1map = mk_map([(kn, "kn"), (kb, "kb"), (kgb, "kgb"), (vb, "vb")])
    h2map = mk_map([(gate, "gate"), (xbf, "xbf"),
                    (Xp[0][1], "Xp1_0"), (Xp[0][2], "Xp2_0"),
                    (Xp[0][4], "Xp4_0"), (Xp[0][8], "Xp8_0"),
                    (Xp[1][1], "Xp1_1"), (Xp[1][2], "Xp2_1"),
                    (Xp[1][4], "Xp4_1"), (Xp[1][8], "Xp8_1"),
                    (Zs[0], "Zs0"), (Zs[1], "Zs1")])[:32]

    for nm in BF_CONSTS:
        O("gpsimd", (lambda e, nm=nm: e.dma_start(K[nm][:, :], cst_in[nm][:, :])),
          w=(nm,), dma=True, fam="c_gp")
    for nm in F32_CONSTS:
        O("sync", (lambda e, nm=nm: e.dma_start(K[nm][:, :], cst_in[nm][:, :])),
          w=(nm,), dma=True, fam="c_sync")
    for nm in S:
        O("sync", (lambda e, nm=nm: e.dma_start(S[nm][:, :], sc_in[nm][:, :])),
          w=(nm,), dma=True, fam="c_sync")
    O("gpsimd", lambda e: e.dma_start(
        wabsb[:, :].rearrange("p (j c) -> p j c", j=8),
        w_in["wab"][:, :].rearrange("(j p) c -> p j c", p=128)),
      w=("wabsb",), dma=True, fam="c_gp")
    O("vector", lambda e: e.memset(ST[:, :], 0.0), w=("ST",))
    O("vector", lambda e: e.memset(STbf[:, :], 0.0), w=("STbf",))
    O("vector", lambda e: e.memset(epsb[:, :], EPS_LN), w=("epsb",))

    SLOTW = 8 * CT
    ppbufs = [(gate, tuple(("gate", b) for b in range(8))),
              (xbf, tuple(("xbf", b) for b in range(8)))]
    ppi = 0
    for nm, din, dout in WEIGHTS:
        dst = nm.replace("T", "S")
        for rr in range(din // 128):
            for c0 in range(0, dout, SLOTW):
                cw = min(SLOTW, dout - c0)
                buf, bres = ppbufs[ppi % 2]
                ppi += 1
                pfam = "g" if ppi % 2 == 1 else "x"
                O("gpsimd", (lambda e, nm=nm, rr=rr, c0=c0, cw=cw, buf=buf:
                             e.dma_start(buf[:, 0:cw],
                                         w_in[nm][128 * rr:128 * rr + 128, c0:c0 + cw])),
                  w=bres, dma=True, fam="ppg" + pfam)
                O("sync", (lambda e, dst=dst, rr=rr, c0=c0, cw=cw, buf=buf:
                           e.dma_start(scr[dst][128 * rr:128 * rr + 128, c0:c0 + cw],
                                       buf[:, 0:cw])),
                  r=bres, w=((dst, rr),), dma=True, fam="ppo" + pfam)

    def ln_stats(src_t, src_res):
        for b in range(8):
            sq = (sqscr, sqscr2)[b % 2]
            sqr = ("sqscr", "sqscr2")[b % 2]
            O("vector", (lambda e, b=b, sq=sq: e.tensor_mul(sq[:, :], blk(src_t, b), blk(src_t, b))),
              r=((src_res, b),), w=(sqr,))
            O("tensor", (lambda e, b=b: e.matmul(pS[0:1, 0:CT], K["c_onescol"][:, :],
                                                 blk(src_t, b), start=(b == 0), stop=(b == 7))),
              r=((src_res, b), "c_onescol"), w=("pS",))
            O("tensor", (lambda e, b=b, sq=sq: e.matmul(pC[0:1, 0:CT], K["c_onescol"][:, :],
                                                 sq[:, :], start=(b == 0), stop=(b == 7))),
              r=(sqr, "c_onescol"), w=RC)
        O("vector", lambda e: e.tensor_single_scalar(rowsA[:, 0:CT], pS[0:1, 0:CT],
                                                     1.0 / DM, ALU.mult),
          r=("pS",), w=("rmean",))
        O("vector", lambda e: e.tensor_mul(rowsB[:, 0:CT], rowsA[:, 0:CT], rowsA[:, 0:CT]),
          r=("rmean",), w=("rmsq",))
        O("vector", lambda e: e.scalar_tensor_tensor(rowsB[:, CT:2 * CT], pC[0:1, 0:CT],
                                                     1.0 / DM, rowsB[:, 0:CT],
                                                     ALU.mult, ALU.subtract),
          r=RC + ("rmsq",), w=("rvar",))
        O("scalar", lambda e: e.activation(rowsA[:, CT:2 * CT], rowsB[:, CT:2 * CT],
                                           AF.Ln, bias=epsb[:, :]),
          r=("rvar", "epsb"), w=("rln",))
        O("scalar", lambda e: e.activation(rowsB[:, 0:CT], rowsA[:, CT:2 * CT],
                                           AF.Exp, scale=-0.5),
          r=("rln",), w=("rrstd",))
        O("tensor", lambda e: e.matmul(pS[0:128, 0:CT], K["c_onesrow"][:, :],
                                       rowsA[:, 0:CT], start=True, stop=True),
          r=("rmean", "c_onesrow"), w=("pS",))
        O("tensor", lambda e: e.matmul(pC[0:128, 0:CT], K["c_onesrow"][:, :],
                                       rowsB[:, 0:CT], start=True, stop=True),
          r=("rrstd", "c_onesrow"), w=RC)
        O("vector", lambda e: e.tensor_copy(mbs[:, :], pS[0:128, 0:CT]), r=("pS",), w=("mbs",))
        O("vector", lambda e: e.tensor_copy(rbs[:, :], pC[0:128, 0:CT]), r=RC, w=("rbs",))

    def ln_apply(src_t, src_res, dst_t, dst_res, resid_t=None, resid_res=None):
        for b in range(8):
            scr2 = sqscr2
            O("vector", (lambda e, b=b: e.tensor_sub(scr2[:, :], blk(src_t, b), mbs[:, :])),
              r=((src_res, b), "mbs"), w=("sqscr2",))
            if resid_t is None:
                O("vector", (lambda e, b=b: e.tensor_mul(blk(dst_t, b), scr2[:, :], rbs[:, :])),
                  r=("sqscr2", "rbs"), w=((dst_res, b),))
            else:
                O("vector", (lambda e, b=b: e.tensor_mul(scr2[:, :], scr2[:, :], rbs[:, :])),
                  r=("sqscr2", "rbs"), w=("sqscr2",))
                O("vector", (lambda e, b=b: e.tensor_add(blk(dst_t, b), scr2[:, :],
                                                         blk(resid_t, b))),
                  r=("sqscr2", (resid_res, b)), w=((dst_res, b),))

    wsl = [0]

    def stream_mm(wS, din, dout, pmaps, mov_ap, mov_res, epi):
        nob = dout // 128
        nsb = din // 512
        G = len(pmaps)
        for og0 in range(0, nob, G):
            ogn = min(G, nob - og0)
            width = ogn * 128
            for sbk in range(nsb):
                slot = wslot[wsl[0] % 3]
                sres = f"wslot{wsl[0] % 3}"
                wsl[0] += 1
                O("sync", (lambda e, sbk=sbk, og0=og0, slot=slot, width=width, wS=wS:
                           e.dma_start(slot[:, 0:4 * width].rearrange("p (jj o) -> p jj o", jj=4),
                                       scr[wS][512 * sbk:512 * sbk + 512,
                                               128 * og0:128 * og0 + width]
                                       .rearrange("(jj p) o -> p jj o", p=128))),
                  r=tuple((wS, 4 * sbk + j) for j in range(4)), w=(sres,), dma=True, fam=sres)
                for jj in range(4):
                    j = 4 * sbk + jj
                    for oi in range(ogn):
                        pap, bres = pmaps[oi]
                        stat = slot[:, jj * width + oi * 128: jj * width + (oi + 1) * 128]
                        O("tensor", (lambda e, pap=pap, stat=stat, mv=mov_ap(j),
                                     first=(sbk == 0 and jj == 0),
                                     last=(sbk == nsb - 1 and jj == 3):
                                     e.matmul(pap, stat, mv, start=first, stop=last)),
                          r=(sres, mov_res(j)), w=bres)
            for oi in range(ogn):
                epi(og0 + oi, *pmaps[oi])

    def pmap4():
        return [(pA[:, 0:CT], ("pA0",)), (pA[:, 512:512 + CT], ("pA1",)),
                (pB[:, 0:CT], ("pB0",)), (pB[:, 512:512 + CT], ("pB1",))]

    def logsig(psrc, psrc_res, bias, dst, dst_res):
        (zap, zr), (sap, sr), (map_, mr) = r2[0], r2[1], r2[2]
        O("vector", (lambda e: e.tensor_copy(zap, psrc)),
          r=(psrc_res,), w=(zr,))
        O("vector", (lambda e: e.tensor_add(zap, zap, S[bias][:, :])),
          r=(zr, bias), w=(zr,))
        O("scalar", lambda e: e.activation(sap, zap, AF.Abs),
          r=(zr,), w=(sr,))
        O("scalar", lambda e: e.activation(sap, sap, AF.Exp, scale=-1.0),
          r=(sr,), w=(sr,))
        O("scalar", lambda e: e.activation(sap, sap, AF.Ln, bias=1.0),
          r=(sr,), w=(sr,))
        O("vector", lambda e: e.tensor_single_scalar(map_, zap, 0.0, ALU.min),
          r=(zr,), w=(mr,))
        O("vector", (lambda e: e.tensor_sub(dst, map_, sap)),
          r=(mr, sr), w=(dst_res,))

    seg = CT // 64

    for ti in range(NTILE):
        ts = ti * CT
        O("gpsimd", (lambda e, ts=ts: e.dma_start(
            xbf[:, :].rearrange("p (b t) -> p b t", b=8),
            xt[:, ts:ts + CT].rearrange("(b p) t -> p b t", p=128))),
          w=tuple(("xbf", b) for b in range(8)), dma=True, fam="xin")

        ln_stats(xbf, "xbf")
        ln_apply(xbf, "xbf", xn, "xn")

        def epi_copy(dst_t, dres):
            def f(o, pap, bres):
                O("scalar", (lambda e, o=o, pap=pap: e.copy(blk(dst_t, o), pap)),
                  r=bres, w=((dres, o),))
            return f

        def epi_silu(dst_t, dres):
            def f(o, pap, bres):
                O("scalar", (lambda e, o=o, pap=pap: e.activation(blk(dst_t, o), pap, AF.Silu)),
                  r=bres, w=((dres, o),))
            return f

        mov_xn = lambda j: blk(xn, j)
        res_xn = lambda j: ("xn", j)
        stream_mm("wqS", DM, DM, pmap4(), mov_xn, res_xn, epi_copy(qg, "qg"))
        stream_mm("wkS", DM, DM, pmap4(), mov_xn, res_xn, epi_copy(kn, "kn"))
        stream_mm("wvS", DM, DM, pmap4(), mov_xn, res_xn, epi_copy(vb, "vb"))
        stream_mm("wgS", DM, DM, pmap4(), mov_xn, res_xn, epi_silu(gate, "gate"))

        for j in range(8):
            O("tensor", (lambda e, j=j: e.matmul(pA[0:16, 0:CT], wabsb[:, j * 32:j * 32 + 16],
                                                 blk(xn, j), start=(j == 0), stop=(j == 7))),
              r=(("xn", j), "wabsb"), w=("pA0",))
            O("tensor", (lambda e, j=j: e.matmul(pB[0:16, 0:CT],
                                                 wabsb[:, j * 32 + 16:j * 32 + 32],
                                                 blk(xn, j), start=(j == 0), stop=(j == 7))),
              r=(("xn", j), "wabsb"), w=("pB0",))

        logsig(pA[0:16, 0:CT], "pA0", "bacol", rla[0], rla[1])
        O("vector", lambda e: e.tensor_single_scalar(rla[0], rla[0], -60.0, ALU.max),
          r=(rla[1],), w=(rla[1],))
        logsig(pB[0:16, 0:CT], "pB0", "bbcol", r2[3][0], r2[3][1])
        O("scalar", lambda e: e.activation(rbr[0], r2[3][0], AF.Exp),
          r=(r2[3][1],), w=(rbr[1],))
        O("vector", lambda e: e.tensor_copy(rG[0], rla[0]),
          r=(rla[1],), w=(rG[1],))
        for sh in (1, 2, 4, 8, 16, 32):
            O("vector", (lambda e, sh=sh: e.tensor_add(
                rgp[0].rearrange("p (s c) -> p s c", c=64)[:, :, sh:64],
                rG[0].rearrange("p (s c) -> p s c", c=64)[:, :, sh:64],
                rG[0].rearrange("p (s c) -> p s c", c=64)[:, :, 0:64 - sh])),
              r=(rG[1],), w=(rgp[1],))
            O("vector", (lambda e, sh=sh: e.tensor_copy(
                rgp[0].rearrange("p (s c) -> p s c", c=64)[:, :, 0:sh],
                rG[0].rearrange("p (s c) -> p s c", c=64)[:, :, 0:sh])),
              r=(rG[1],), w=(rgp[1],))
            O("vector", lambda e: e.tensor_copy(rG[0], rgp[0]),
              r=(rgp[1],), w=(rG[1],))
        O("scalar", lambda e: e.activation(rgv[0], rG[0], AF.Exp),
          r=(rG[1],), w=(rgv[1],))
        gend = rG[0].rearrange("p (s c) -> p s c", c=64)[:, :, 63:64] \
            .broadcast_to([16, seg, 64])
        O("vector", lambda e: e.tensor_sub(
            rgr[0].rearrange("p (s c) -> p s c", c=64),
            rG[0].rearrange("p (s c) -> p s c", c=64), gend),
          r=(rG[1],), w=(rgr[1],))
        O("scalar", lambda e: e.activation(rgr[0], rgr[0], AF.Exp, scale=-1.0),
          r=(rgr[1],), w=(rgr[1],))
        O("vector", lambda e: e.tensor_copy(rowg[:, 0:CT], rG[0]),
          r=(rG[1],), w=("Ghib",))
        O("vector", lambda e: e.tensor_sub(rowg[:, CT:2 * CT], rG[0],
                                           rowg[:, 0:CT]),
          r=(rG[1], "Ghib"), w=("Glob",))
        gvend = rgv[0].rearrange("p (s c) -> p s c", c=64)[:, :, 63:64] \
            .broadcast_to([16, seg, 8])
        O("vector", (lambda e: e.tensor_mul(
            mov2[0][:, 0:8 * seg].rearrange("p (s c) -> p s c", c=8), gvend,
            K["c_pairrep"][:, 0:8 * seg].rearrange("p (s c) -> p s c", c=8))),
          r=(rgv[1], "c_pairrep"), w=(("m2", 0),))
        O("tensor", lambda e: e.matmul(pS[0:128, 0:8 * seg], K["c_parity"][:, :],
                                       mov2[0][:, 0:8 * seg], start=True, stop=True),
          r=(("m2", 0), "c_parity"), w=("pS",))
        O("vector", lambda e: e.tensor_copy(gC[:, 0:8 * seg], pS[0:128, 0:8 * seg]),
          r=("pS",), w=("gC",))

        for raw_t, raw_res, pap, bres, dst, dres in (
                (qg, "qg", pS[0:16, 0:CT], ("pS",), r2[0][0], r2[0][1]),
                (kn, "kn", pC[0:16, 0:CT], RC, r2[1][0], r2[1][1])):
            for b in range(8):
                sq = (sqscr, sqscr2)[b % 2]
                sqr = ("sqscr", "sqscr2")[b % 2]
                O("vector", (lambda e, b=b, raw_t=raw_t, sq=sq: e.tensor_mul(
                    sq[:, :], blk(raw_t, b), blk(raw_t, b))),
                  r=((raw_res, b),), w=(sqr,))
                O("tensor", (lambda e, b=b, pap=pap, sq=sq: e.matmul(
                    pap, K["c_sq16"][:, b * 16:b * 16 + 16], sq[:, :],
                    start=(b == 0), stop=(b == 7))),
                  r=(sqr, "c_sq16"), w=bres)
            O("scalar", (lambda e, pap=pap, dst=dst: e.activation(dst, pap, AF.Ln)),
              r=bres, w=(dres,))
            O("scalar", (lambda e, dst=dst: e.activation(dst, dst, AF.Exp, scale=-0.5)),
              r=(dres,), w=(dres,))
            O("vector", (lambda e, dst=dst: e.tensor_single_scalar(dst, dst, 1e6, ALU.min)),
              r=(dres,), w=(dres,))
        O("vector", lambda e: e.tensor_mul(r2[2][0], r2[0][0], rgv[0]),
          r=(r2[0][1], rgv[1]), w=(r2[2][1],))
        O("vector", lambda e: e.scalar_tensor_tensor(r2[3][0], r2[1][0],
                                                     -1.0, rbr[0],
                                                     ALU.mult, ALU.mult),
          r=(r2[1][1], rbr[1]), w=(r2[3][1],))
        O("vector", lambda e: e.tensor_mul(r2[4][0], r2[3][0], rgv[0]),
          r=(r2[3][1], rgv[1]), w=(r2[4][1],))
        rowsrc = [(r2[0][1], r2[0][0], rowbA[:, 0:CT]),
                  (r2[2][1], r2[2][0], rowbA[:, CT:2 * CT]),
                  (r2[1][1], r2[1][0], rowbA[:, 2 * CT:3 * CT]),
                  (r2[3][1], r2[3][0], rowbA[:, 3 * CT:4 * CT]),
                  (r2[4][1], r2[4][0], rowbB[:, 0:CT]),
                  (rbr[1], rbr[0], rowbB[:, CT:2 * CT])]
        for res, src, dst in rowsrc:
            O("vector", (lambda e, src=src, dst=dst: e.tensor_copy(dst, src)),
              r=(res,), w=((res, "bf"),))
        rowap = {}
        rowres = {}
        for nm, (res, _, dst) in zip(("rnq", "rg", "rnk", "rkb", "rkg", "brow"), rowsrc):
            rowap[nm] = dst
            rowres[nm] = (res, "bf")

        for b in range(8):
            def bc(res, pap, bres):
                O("tensor", (lambda e, res=res, pap=pap, b=b: e.matmul(
                    pap, K["c_b16"][:, b * 128:(b + 1) * 128], rowap[res],
                    start=True, stop=True)),
                  r=(rowres[res], "c_b16"), w=bres)
            bc("rnq", pS[0:128, 0:CT], ("pS",))
            O("vector", (lambda e, b=b: e.scalar_tensor_tensor(
                blk(qn, b), pS[0:128, 0:CT], 1.0, blk(qg, b), ALU.mult, ALU.mult)),
              r=("pS", ("qg", b)), w=(("qn", b),))
            bc("rg", pC[0:128, 0:CT], RC)
            O("vector", (lambda e, b=b: e.scalar_tensor_tensor(
                blk(qg, b), pC[0:128, 0:CT], 1.0, blk(qg, b), ALU.mult, ALU.mult)),
              r=RC + (("qg", b),), w=(("qg", b),))
            bc("rkb", pS[0:128, 0:CT], ("pS",))
            O("vector", (lambda e, b=b: e.scalar_tensor_tensor(
                blk(kb, b), pS[0:128, 0:CT], 1.0, blk(kn, b), ALU.mult, ALU.mult)),
              r=("pS", ("kn", b)), w=(("kb", b),))
            bc("rkg", pC[0:128, 0:CT], RC)
            O("vector", (lambda e, b=b: e.scalar_tensor_tensor(
                blk(kgb, b), pC[0:128, 0:CT], 1.0, blk(kn, b), ALU.mult, ALU.mult)),
              r=RC + (("kn", b),), w=(("kgb", b),))
            bc("rnk", pS[0:128, 0:CT], ("pS",))
            O("vector", (lambda e, b=b: e.scalar_tensor_tensor(
                blk(kn, b), pS[0:128, 0:CT], 1.0, blk(kn, b), ALU.mult, ALU.mult)),
              r=("pS", ("kn", b)), w=(("kn", b),))
            bc("brow", pC[0:128, 0:CT], RC)
            O("vector", (lambda e, b=b: e.scalar_tensor_tensor(
                blk(vb, b), pC[0:128, 0:CT], 1.0, blk(vb, b), ALU.mult, ALU.mult)),
              r=RC + (("vb", b),), w=(("vb", b),))

        def gen_SF(d, bi):
            t0 = d * 128
            XpB, P0sB, SMTbB = Xp[bi], P0s2[bi], SMTb2[bi]
            rs = lambda nm: f"{nm}_{bi}"

            def hsl(t8, h, c):
                return blk(t8, h // 2)[64 * (h % 2):64 * (h % 2) + 64,
                                       t0 + 64 * c: t0 + 64 * c + 64]

            O("tensor", (lambda e, t0=t0: e.transpose(
                pS[0:128, 0:16], rowf1[:, 2 * CT + t0:2 * CT + t0 + 128],
                K["c_id32"][0:16, 0:16])),
              r=(rgr[1], "c_id32"), w=("pS",))
            O("vector", (lambda e, SMTbB=SMTbB: e.tensor_copy(SMTbB[:, 0:16], pS[0:128, 0:16])),
              r=("pS",), w=(rs("SMTb"),))
            for g2 in range(2):
                for rres, rap in (("Ghib", rowg[:, t0:t0 + 128]),
                                  ("Glob", rowg[:, CT + t0:CT + t0 + 128])):
                    O("tensor", (lambda e, g2=g2, rap=rap, first=(rres == "Ghib"):
                                 e.matmul(pA[:, 512 * g2:512 * g2 + 512], rap,
                                          K["c_ind64"][:, 512 * g2:512 * g2 + 512],
                                          start=first, stop=False)),
                      r=(rres, "c_ind64"), w=(RA[g2],))
            yield
            for half, nmsk in ((0, "c_nonel"), (1, "c_noneh")):
                for hl, rres in ((0, "Ghib"), (1, "Glob")):
                    c0r = (0, CT)[hl]
                    src = rowg[:, c0r + t0 + 64 * half: c0r + t0 + 64 * half + 64] \
                        .unsqueeze(1).broadcast_to([16, 16, 64])
                    O("vector", (lambda e, src=src: e.tensor_mul(
                        mov2[0][:, :].rearrange("p (h c) -> p h c", c=64),
                        K["c_ind64"][:, :].rearrange("p (h c) -> p h c", c=64), src)),
                      r=(rres, "c_ind64"), w=(("m2", 0),))
                    for g2 in range(2):
                        last = (half == 1 and hl == 1)
                        O("tensor", (lambda e, g2=g2, nmsk=nmsk, last=last:
                                     e.matmul(pA[:, 512 * g2:512 * g2 + 512], K[nmsk][:, :],
                                              mov2[0][:, 512 * g2:512 * g2 + 512],
                                              start=False, stop=last)),
                          r=(("m2", 0), nmsk), w=(RA[g2],))
            yield
            O("vector", lambda e: e.tensor_single_scalar(escr[:, :], pA[:, :], 0.0, ALU.min),
              r=RA, w=("escr",))
            O("scalar", lambda e: e.activation(Eun[:, :], escr[:, :], AF.Exp),
              r=("escr",), w=("EunR",))
            O("vector", lambda e: e.tensor_scalar(escr[:, :], pA[:, :], -1.0, 0.0,
                                                  ALU.mult, ALU.min),
              r=RA, w=("escr",))
            O("scalar", lambda e: e.activation(Eprs[:, :], escr[:, :], AF.Exp),
              r=("escr",), w=("EprR",))
            O("vector", lambda e: e.tensor_mul(
                Epri[:, :].rearrange("p (h c) -> p h c", c=64),
                Eprs[:, :].rearrange("p (h c) -> p h c", c=64),
                K["c_maskpri"][:, :].unsqueeze(1).broadcast_to([128, 16, 64])),
              r=("EprR", "c_maskpri"), w=("Epri",))
            O("vector", lambda e: e.tensor_mul(
                Eprs[:, :].rearrange("p (h c) -> p h c", c=64),
                Eprs[:, :].rearrange("p (h c) -> p h c", c=64),
                K["c_maskpr"][:, :].unsqueeze(1).broadcast_to([128, 16, 64])),
              r=("EprR", "c_maskpr"), w=("Eprs",))
            O("vector", lambda e: e.tensor_mul(
                Eun[:, :].rearrange("p (h c) -> p h c", c=64),
                Eun[:, :].rearrange("p (h c) -> p h c", c=64),
                K["c_maskun"][:, :].unsqueeze(1).broadcast_to([128, 16, 64])),
              r=("EunR", "c_maskun"), w=("Eun",))
            yield
            ablocks = [(0, 0), (0, 64), (64, 64), (64, 0)]
            for akind, (s1f, s2f, evt, evr) in enumerate((
                    (kb, kn, Aun_s, "Aun_s"), (kn, kb, Aprs, "Aprs"),
                    (kn, qn, Akqs, "Akqs"))):
                s1r = ("kb", "kn", "kn")[akind]
                s2r = ("kn", "kb", "qn")[akind]
                for bi2, (hb, cb) in enumerate(ablocks):
                    c = cb // 64
                    first = True
                    for h in range(hb // 64, H, 2):
                        tp = (hb, cb) if (hb or cb) else None
                        ps = (bi2 == 2) and first
                        first = False
                        O("tensor", (lambda e, sA=hsl(s1f, h, c), m=hsl(s2f, h, c),
                                     h=h, cb=cb, tp=tp:
                                     e.matmul(pA[cb:cb + 64, 64 * h:64 * h + 64], sA, m,
                                              start=True, stop=True, tile_position=tp)),
                          r=((s1r, h // 2), (s2r, h // 2)),
                          w=(RA[0],) if h < 8 else (RA[1],), pe_sync=ps)
                ev_eng = ("vector", "scalar", "vector")[akind]
                fn = (lambda e, evt=evt: e.tensor_copy(evt[:, :], pA[:, :])) \
                    if ev_eng == "vector" else \
                    (lambda e, evt=evt: e.copy(evt[:, :], pA[:, :]))
                O(ev_eng, fn, r=RA, w=(evr,))
                yield
            O("vector", (lambda e, XpB=XpB: e.tensor_mul(XpB[1][:, :], Aprs[:, :], Eprs[:, :])),
              r=("Aprs", "Eprs"), w=(rs("Xp1"),))
            O("vector", lambda e: e.tensor_mul(Xua[:, :], Aun_s[:, :], Eun[:, :]),
              r=("Aun_s", "Eun"), w=("Xua",))
            O("vector", (lambda e, P0sB=P0sB: e.tensor_mul(P0sB[:, :], Akqs[:, :], Epri[:, :])),
              r=("Akqs", "Epri"), w=(rs("P0s"),))
            yield
            prevp, prevpt = rs("Xp1"), XpB[1]
            prevu, prevut = "Xua", Xua
            for m in (2, 4, 8, 16, 32):
                for h in range(H):
                    for c in range(2):
                        cb = 64 * c
                        tp = (cb, cb) if cb else None
                        O("tensor", (lambda e, h=h, cb=cb, tp=tp, su=prevut, sp=prevpt:
                                     e.matmul(pA[cb:cb + 64, 64 * h:64 * h + 64],
                                              su[cb:cb + 64, 64 * h:64 * h + 64],
                                              sp[cb:cb + 64, 64 * h:64 * h + 64],
                                              start=True, stop=True, tile_position=tp)),
                          r=(prevu, prevp), w=(RA[0],) if h < 8 else (RA[1],))
                O("vector", (lambda e, m=m, XpB=XpB: e.tensor_copy(XpB[m][:, :], pA[:, :])),
                  r=RA, w=(rs(f"Xp{m}"),))
                yield
                if m != 32:
                    for h in range(H):
                        for c in range(2):
                            cb = 64 * c
                            tp = (cb, cb) if cb else None
                            O("tensor", (lambda e, h=h, cb=cb, tp=tp, su=prevut, sp=prevpt:
                                         e.matmul(pA[cb:cb + 64, 64 * h:64 * h + 64],
                                                  sp[cb:cb + 64, 64 * h:64 * h + 64],
                                                  su[cb:cb + 64, 64 * h:64 * h + 64],
                                                  start=True, stop=True, tile_position=tp)),
                              r=(prevu, prevp), w=(RA[0],) if h < 8 else (RA[1],))
                    nxt = Xub if prevut is Xua else Xua
                    nxtr = "Xub" if prevut is Xua else "Xua"
                    O("scalar", (lambda e, nxt=nxt: e.copy(nxt[:, :], pA[:, :])),
                      r=RA, w=(nxtr,))
                    prevu, prevut = nxtr, nxt
                    yield
                prevp, prevpt = rs(f"Xp{m}"), XpB[m]

        def gen_SEQ(d, bi):
            t0 = d * 128
            XpB, P0sB, SMTbB = Xp[bi], P0s2[bi], SMTb2[bi]
            VbTsB, KtoksB = VbTs, Ktoks
            rs = lambda nm: f"{nm}_{bi}"

            def hsl(t8, h, c):
                return blk(t8, h // 2)[64 * (h % 2):64 * (h % 2) + 64,
                                       t0 + 64 * c: t0 + 64 * c + 64]

            for src_t, sres, dst, dres in ((vb, "vb", VbTsB, "VbTs"),
                                           (kn, "kn", KtoksB, "Ktoks")):
                for pr in range(8):
                    for c in range(2):
                        cb = 64 * c
                        src = blk(src_t, pr)[:, t0 + 64 * c: t0 + 64 * c + 64]
                        O("tensor", (lambda e, src=src, pr=pr, cb=cb: e.transpose(
                            ptr[cb:cb + 64, 128 * pr:128 * pr + 128], src,
                            K["c_idbf"][:, :],
                            tile_position=(0, cb) if cb else None)),
                          r=((sres, pr), "c_idbf"), w=("ptr",))
                O("vector", (lambda e, dst=dst: e.tensor_copy(dst[:, :], ptr[:, :])),
                  r=("ptr",), w=(dres,))
                yield
            for c in range(2):
                cb = 64 * c
                cg = 2 * d + c
                for g2 in range(2):
                    O("tensor", (lambda e, cb=cb, g2=g2, VbTsB=VbTsB: e.matmul(
                        pB[cb:cb + 64, 512 * g2:512 * g2 + 512],
                        K["c_idbf"][cb:cb + 64, cb:cb + 64],
                        VbTsB[cb:cb + 64, 512 * g2:512 * g2 + 512],
                        start=True, stop=False, skip_group_check=True,
                        tile_position=(cb, cb) if cb else None)),
                      r=("VbTs", "c_idbf"), w=(RB[g2],))
                for par in (0, 1):
                    first = True
                    for h in range(par, H, 2):
                        hb = 64 * par
                        pr = h // 2
                        ps = first
                        first = False
                        O("tensor", (lambda e, sA=hsl(kgb, h, c), hb=hb, pr=pr, cb=cb, h=h:
                                     e.matmul(pB[cb:cb + 64, 64 * h:64 * h + 64], sA,
                                              STbf[hb:hb + 64, 64 * pr:64 * pr + 64],
                                              start=False, stop=False, skip_group_check=True,
                                              tile_position=(hb, cb) if (hb or cb) else None)),
                          r=(("kgb", h // 2), "STbf"), w=(RB[0],) if h < 8 else (RB[1],),
                          pe_sync=ps)
                O("vector", (lambda e, cb=cb: e.tensor_copy(Zs[0][cb:cb + 64, :],
                                                            pB[cb:cb + 64, :])),
                  r=RB, w=("Zs0",))
                yield
                zi = 0
                for m in (1, 2, 4, 8, 16, 32):
                    mt = XpB[m]
                    for h in range(H):
                        O("tensor", (lambda e, mt=mt, h=h, cb=cb, zi=zi, m=m:
                                     e.matmul(pB[cb:cb + 64, 64 * h:64 * h + 64],
                                              mt[cb:cb + 64, 64 * h:64 * h + 64],
                                              Zs[zi][cb:cb + 64, 64 * h:64 * h + 64],
                                              start=False, stop=False, skip_group_check=True,
                                              tile_position=(cb, cb) if cb else None)),
                          r=(rs(f"Xp{m}"), f"Zs{zi}"), w=(RB[0],) if h < 8 else (RB[1],))
                    zi = 1 - zi
                    O("vector", (lambda e, cb=cb, zi=zi: e.tensor_copy(
                        Zs[zi][cb:cb + 64, :], pB[cb:cb + 64, :])),
                      r=RB, w=(f"Zs{zi}",))
                    yield
                O("vector", (lambda e, cb=cb, zi=zi, SMTbB=SMTbB: e.tensor_mul(
                    Wgt[cb:cb + 64, :].rearrange("p (h i) -> p h i", i=64),
                    Zs[zi][cb:cb + 64, :].rearrange("p (h i) -> p h i", i=64),
                    SMTbB[cb:cb + 64, 0:16].unsqueeze(2).broadcast_to([64, 16, 64]))),
                  r=(f"Zs{zi}", rs("SMTb")), w=("Wgt",))
                for h in range(H):
                    hb = 64 * (h % 2)
                    pr = h // 2
                    col = 512 * c + 64 * pr
                    O("tensor", (lambda e, hb=hb, pr=pr, col=col, m=hsl(qg, h, c), h=h:
                                 e.matmul(pC[hb:hb + 64, col:col + 64],
                                          STbf[hb:hb + 64, 64 * pr:64 * pr + 64], m,
                                          start=(h <= 1), stop=False, skip_group_check=True,
                                          tile_position=(hb, hb) if hb else None)),
                      r=("STbf", ("qg", h // 2)), w=(RC[c],))
                yield
                for h in range(H):
                    hb = 64 * (h % 2)
                    pr = h // 2
                    col = 512 * c + 64 * pr
                    O("tensor", (lambda e, hb=hb, cb=cb, col=col, zi=zi, h=h, P0sB=P0sB:
                                 e.matmul(pC[hb:hb + 64, col:col + 64],
                                          Zs[zi][cb:cb + 64, 64 * h:64 * h + 64],
                                          P0sB[cb:cb + 64, 64 * h:64 * h + 64],
                                          start=False, stop=(h >= H - 2), skip_group_check=True,
                                          tile_position=(cb, hb) if (cb or hb) else None)),
                      r=(f"Zs{zi}", rs("P0s")), w=(RC[c],))
                yield
                for h in range(H):
                    hb = 64 * (h % 2)
                    pr = h // 2
                    O("tensor", (lambda e, hb=hb, pr=pr, cb=cb, h=h, KtoksB=KtoksB: e.matmul(
                        pS[hb:hb + 64, 64 * pr:64 * pr + 64],
                        KtoksB[cb:cb + 64, 128 * pr + hb:128 * pr + hb + 64],
                        Wgt[cb:cb + 64, 64 * h:64 * h + 64],
                        start=(h <= 1), stop=(h >= H - 2), skip_group_check=True,
                        tile_position=(cb, hb) if (cb or hb) else None)),
                      r=("Ktoks", "Wgt"), w=("pS",))
                O("vector", (lambda e, cg=cg: e.tensor_mul(
                    stsc[:, :].rearrange("p (r i) -> p r i", i=64),
                    ST[:, :].rearrange("p (r i) -> p r i", i=64),
                    gC[:, cg * 8:cg * 8 + 8].unsqueeze(2).broadcast_to([128, 8, 64]))),
                  r=("ST", "gC"), w=("stsc",))
                O("vector", lambda e: e.tensor_add(ST[:, :], stsc[:, :], pS[0:128, 0:512]),
                  r=("stsc", "pS"), w=("ST",))
                O("vector", lambda e: e.tensor_copy(STbf[:, :], ST[:, :]),
                  r=("ST",), w=("STbf",))
                ogap = gate[:, :].rearrange("p (pr tt) -> p pr tt", pr=8)[
                    :, :, t0 + 64 * c: t0 + 64 * c + 64]
                pcv = pC[:, 512 * c:512 * c + 512].rearrange("p (pr t) -> p pr t", pr=8)
                O("vector", (lambda e, ogap=ogap, pcv=pcv: e.scalar_tensor_tensor(
                    ogap, pcv, 1.0, ogap, ALU.mult, ALU.mult)),
                  r=(RC[c],) + tuple(("gate", b) for b in range(8)),
                  w=tuple(("gate", b) for b in range(8)))
                yield

        def drive(gens):
            gens = [g for g in gens if g is not None]
            while gens:
                for g in list(gens):
                    try:
                        next(g)
                    except StopIteration:
                        gens.remove(g)

        prev_seq = None
        if "delta" not in SKIP:
            for d in range(NDBL):
                sf = gen_SF(d, d % 2)
                drive([prev_seq, sf])
                prev_seq = gen_SEQ(d, d % 2)
            drive([prev_seq])

        stream_mm("woS", DM, DM, pmap4(), lambda j: blk(gate, j), lambda j: ("gate", j),
                  epi_copy(vb, "vb"))
        ln_stats(vb, "vb")
        ln_apply(vb, "vb", xn, "xn", resid_t=xbf, resid_res="xbf")

        ln_stats(xn, "xn")
        ln_apply(xn, "xn", qn, "qn")

        mov_yn = lambda j: blk(qn, j)
        res_yn = lambda j: ("qn", j)
        nob = DF // 128
        for og0 in range(0, nob, 2):
            for wS, pmaps in (("w1S", [(pA[:, 0:CT], ("pA0",)), (pA[:, 512:512 + CT], ("pA1",))]),
                              ("w2S", [(pC[:, 0:CT], ("pC0",)), (pC[:, 512:512 + CT], ("pC1",))])):
                slot = wslot[wsl[0] % 3]
                sres = f"wslot{wsl[0] % 3}"
                wsl[0] += 1
                O("sync", (lambda e, og0=og0, slot=slot, wS=wS:
                           e.dma_start(slot[:, 0:2048].rearrange("p (jj o) -> p jj o", jj=8),
                                       scr[wS][0:1024, 128 * og0:128 * og0 + 256]
                                       .rearrange("(jj p) o -> p jj o", p=128))),
                  r=tuple((wS, j) for j in range(8)), w=(sres,), dma=True, fam=sres)
                for j in range(8):
                    for oi in range(2):
                        pap, bres = pmaps[oi]
                        stat = slot[:, j * 256 + oi * 128: j * 256 + (oi + 1) * 128]
                        O("tensor", (lambda e, pap=pap, stat=stat, mv=mov_yn(j),
                                     first=(j == 0), last=(j == 7):
                                     e.matmul(pap, stat, mv, start=first, stop=last)),
                          r=(sres, res_yn(j)), w=bres)
            for oi in range(2):
                o = og0 + oi
                papA = (pA[:, 0:CT], pA[:, 512:512 + CT])[oi]
                papC = (pC[:, 0:CT], pC[:, 512:512 + CT])[oi]
                brA = (("pA0",), ("pA1",))[oi]
                brC = (("pC0",), ("pC1",))[oi]
                uap, ures = u1map[o]
                O("scalar", (lambda e, o=o, papA=papA: e.activation(
                    sqscr[:, :], papA, AF.Silu, scale=S["s1s"][:, o:o + 1])),
                  r=brA + ("s1s",), w=("sqscr",))
                O("vector", (lambda e, o=o, papC=papC, uap=uap: e.scalar_tensor_tensor(
                    uap, papC, S["s2s"][:, o:o + 1], sqscr[:, :], ALU.mult, ALU.mult)),
                  r=brC + ("s2s", "sqscr"), w=(ures,))

        def epi_gelu(dmap, scal):
            def f(o, pap, bres):
                dap, dres = dmap[o]
                O("scalar", (lambda e, o=o, pap=pap, dap=dap: e.activation(
                    dap, pap, AF.Gelu, scale=S[scal][:, o:o + 1])),
                  r=bres + (scal,), w=(dres,))
            return f

        stream_mm("h1S", DF, DF, pmap4(), lambda j: u1map[j][0], lambda j: u1map[j][1],
                  epi_gelu(h2map, "hs1"))
        stream_mm("h2S", DF, DF, pmap4(), lambda j: h2map[j][0], lambda j: h2map[j][1],
                  epi_gelu(u1map, "hs2"))

        def epi_w3(o, pap, bres):
            O("scalar", (lambda e, o=o, pap=pap: e.mul(blk(qg, o), pap, S["w3s"][:, o:o + 1])),
              r=bres + ("w3s",), w=(("qg", o),))

        stream_mm("w3S", DF, DM, pmap4(), lambda j: u1map[j][0], lambda j: u1map[j][1],
                  epi_w3)

        ln_stats(qg, "qg")
        for b in range(8):
            ob = outb[b % 2]
            obres = f"outb{b % 2}"
            O("vector", (lambda e, b=b: e.tensor_sub(sqscr2[:, :], blk(qg, b), mbs[:, :])),
              r=(("qg", b), "mbs"), w=("sqscr2",))
            O("vector", (lambda e, b=b: e.tensor_mul(sqscr2[:, :], sqscr2[:, :], rbs[:, :])),
              r=("sqscr2", "rbs"), w=("sqscr2",))
            O("vector", (lambda e, b=b, ob=ob: e.tensor_add(ob[:, :], sqscr2[:, :],
                                                            blk(xn, b))),
              r=("sqscr2", ("xn", b)), w=(obres,))
            O("sync", (lambda e, b=b, ob=ob, ts=ts: e.dma_start(
                yt[128 * b:128 * b + 128, ts:ts + CT], ob[:, :])),
              r=(obres,), w=(("ydone", b % 2),), dma=True, fam=f"yo{b % 2}")

    O("sync", lambda e: e.nop(), r=(("ydone", 0), ("ydone", 1)))

    with nc.Block() as block:
        em.finalize(nc, block)
    return nc


_cache = {}


def prep_inputs(inputs, CT=512):
    cc = host_constants()
    base = {}
    for nm, src in [("wqT", "L0_attn_Wq"), ("wkT", "L0_attn_Wk"), ("wvT", "L0_attn_Wv"),
                    ("wgT", "L0_attn_Wg"), ("woT", "L0_attn_Wo"),
                    ("w1T", "L0_ffn_w1_w"), ("w2T", "L0_ffn_w2_w"),
                    ("h1T", "L0_ffn_h1_w"), ("h2T", "L0_ffn_h2_w"),
                    ("w3T", "L0_ffn_w3_w")]:
        base[nm] = np.ascontiguousarray(np.asarray(inputs[src], np.float32).T)
    base["wab"] = np.ascontiguousarray(np.concatenate(
        [np.asarray(inputs["L0_attn_Wa"], np.float32).T,
         np.asarray(inputs["L0_attn_Wb"], np.float32).T], axis=1))
    for nm, src in [("s1s", "L0_ffn_w1_s"), ("s2s", "L0_ffn_w2_s"),
                    ("hs1", "L0_ffn_h1_s"), ("hs2", "L0_ffn_h2_s")]:
        base[nm] = np.ascontiguousarray(np.asarray(inputs[src], np.float32).reshape(32, 128).T)
    base["w3s"] = np.ascontiguousarray(np.asarray(inputs["L0_ffn_w3_s"], np.float32).reshape(8, 128).T)
    base["bacol"] = np.ascontiguousarray(np.broadcast_to(
        np.asarray(inputs["L0_attn_ba"], np.float32).reshape(16, 1), (16, CT)))
    base["bbcol"] = np.ascontiguousarray(np.broadcast_to(
        np.asarray(inputs["L0_attn_bb"], np.float32).reshape(16, 1), (16, CT)))
    for nm in BF_CONSTS + F32_CONSTS:
        base[nm] = cc[nm]
    return base


def kernel(**inputs):
    T, CT = 8192, 512
    if "prog" not in _cache:
        _cache["prog"] = build_program(T, CT)
    nc = _cache["prog"]
    base = prep_inputs(inputs)
    x = np.asarray(inputs["L0_attn_x"], np.float32)
    in_maps = []
    for c in range(NCORES):
        m = dict(base)
        m["xt"] = np.ascontiguousarray(x[c].T)
        in_maps.append(m)
    res = run_bass_kernel_spmd(nc, in_maps, core_ids=list(range(NCORES)))
    out = np.stack([np.asarray(r["yt"]).T for r in res.results], axis=0)
    return out.astype(np.float32)
